# version 02
# baseline (speedup 1.0000x reference)
import numpy as np
import ml_dtypes
import concourse.bass as bass
import concourse.mybir as mybir
from concourse.bass_utils import run_bass_kernel_spmd
from concourse.bass_core_rust import IndirectOffsetOnAxis

F32 = mybir.dt.float32
BF16 = mybir.dt.bfloat16
I32 = mybir.dt.int32
AF = mybir.ActivationFunctionType
ALU = mybir.AluOpType
AX = mybir.AxisListType.X

D_MODEL = 1024
D_INNER = 2048
D_STATE = 16
D_CONV = 4
SEQ = 4096
BATCH = 16
N_CORES = 8
SEQ_PER_CORE = BATCH // N_CORES
TOK = SEQ_PER_CORE * SEQ
NC_D = D_MODEL // 128
NC_I = D_INNER // 128
Q = 512
NTT = Q // 128
NCHUNK = SEQ // Q
E = 16
NG, EPG, TKG, TOPK = 4, 4, 2, 4
MOE_SCALE = 2.5
FB = 512
NB = (TOPK * TOK) // FB + (E - 1)
NSLOT = NB * FB
LN_EPS = 1e-5

ENGINES = ("pe", "act", "dve", "pool", "sp")
ENG_ATTR = {"pe": "tensor", "act": "scalar", "dve": "vector",
            "pool": "gpsimd", "sp": "sync"}


class Op:
    __slots__ = ("idx", "eng", "fn", "reads", "writes", "is_dma", "deps",
                 "sig_id", "tag")

    def __init__(self, idx, eng, fn, reads, writes, is_dma, tag):
        self.idx = idx
        self.eng = eng
        self.fn = fn
        self.reads = reads
        self.writes = writes
        self.is_dma = is_dma
        self.deps = set()
        self.sig_id = None
        self.tag = tag


class Sched:
    def __init__(self):
        self.ops = []
        self.last_writer = {}
        self.readers = {}
        self.keymap = {}

    def op(self, eng, fn, r=(), w=(), dma=False, tag=""):
        km = self.keymap

        def expand(keys):
            out = []
            for k in keys:
                v = km.get(k, k)
                if isinstance(v, (tuple, list)):
                    out.extend(v)
                else:
                    out.append(v)
            return tuple(out)
        o = Op(len(self.ops), eng, fn, expand(r), expand(w), dma, tag)
        for k in o.reads:
            lw = self.last_writer.get(k)
            if lw is not None:
                o.deps.add(lw.idx)
            self.readers.setdefault(k, []).append(o)
        for k in o.writes:
            lw = self.last_writer.get(k)
            if lw is not None:
                o.deps.add(lw.idx)
            for rd in self.readers.get(k, ()):
                o.deps.add(rd.idx)
            self.readers[k] = []
            self.last_writer[k] = o
        o.deps.discard(o.idx)
        self.ops.append(o)
        return o

    def emit(self, nc, final_outputs):
        ops = self.ops
        needs_sig = [o.is_dma for o in ops]
        for o in ops:
            for d in o.deps:
                dep = ops[d]
                if (dep.eng == o.eng == "pe" and not dep.is_dma
                        and not o.is_dma):
                    continue
                needs_sig[d] = True
        for o in final_outputs:
            needs_sig[o.idx] = True

        DMA_LANES = 8
        counters = {}
        qord = {}
        for o in ops:
            if needs_sig[o.idx]:
                if o.is_dma:
                    n = qord.get(o.eng, 0)
                    qord[o.eng] = n + 1
                    key = ("d", o.eng, n % DMA_LANES)
                else:
                    key = ("c", o.eng)
                counters[key] = counters.get(key, 0) + 1
                o.sig_id = (key, counters[key])

        sems, ctxs = {}, []
        for key in counters:
            cm = nc.semaphore("sem_" + "_".join(str(k) for k in key))
            sems[key] = cm.__enter__()
            ctxs.append(cm)

        block_cm = nc.Block()
        block = block_cm.__enter__()
        per_eng = {e: [o for o in ops if o.eng == e] for e in ENGINES}
        for eng in ENGINES:
            eng_ops = per_eng[eng]
            if not eng_ops:
                continue

            def body(engine, eng=eng, eng_ops=eng_ops):
                water = {}
                for o in eng_ops:
                    want = {}
                    if o.is_dma and o.sig_id is not None:
                        key, val = o.sig_id
                        if val > 1:
                            want[key] = val - 1
                    for d in sorted(o.deps):
                        dep = ops[d]
                        if dep.sig_id is None:
                            continue
                        key, val = dep.sig_id
                        want[key] = max(want.get(key, 0), val)
                    for key, val in want.items():
                        v = val * 16 if key[0] == "d" else val
                        if water.get(key, 0) >= v:
                            continue
                        engine.wait_ge(sems[key], v)
                        water[key] = v
                    inst = o.fn(engine)
                    if o.sig_id is not None:
                        key, val = o.sig_id
                        if inst is None:
                            raise RuntimeError(f"op {o.tag}: no instruction")
                        inst.then_inc(sems[key], 16 if key[0] == "d" else 1)

            getattr(block, ENG_ATTR[eng])(body)

        def final_body(engine):
            want = {}
            for o in final_outputs:
                key, val = o.sig_id
                v = val * 16 if key[0] == "d" else val
                want[key] = max(want.get(key, 0), v)
            for key, v in want.items():
                engine.wait_ge(sems[key], v)
        block.sync(final_body)

        block_cm.__exit__(None, None, None)
        for cm in reversed(ctxs):
            cm.__exit__(None, None, None)
        return sum(needs_sig), len(ops)


def host_constants():
    c = {}
    c["ident_f32"] = np.eye(128, dtype=np.float32)
    c["ident_bf"] = np.eye(128, dtype=np.float32).astype(ml_dtypes.bfloat16)
    c["ones_row"] = np.ones((1, 128), dtype=np.float32)
    c["ltri16"] = np.triu(np.ones((16, 16), np.float32), 1)
    c["ltri16i"] = np.triu(np.ones((16, 16), np.float32), 0)
    c["io999"] = 999.0 - np.tile(np.arange(16, dtype=np.float32)[None, :],
                                 (128, 1))
    c["iota_p"] = np.arange(128, dtype=np.float32)[:, None]
    c["iota16"] = np.tile(np.arange(16, dtype=np.float32)[None, :], (128, 1))
    c["iota_nb"] = np.arange(NB, dtype=np.float32)[None, :]
    sel = np.zeros((33, 33 * 128), np.float32)
    for r in range(33):
        sel[r, r * 128:(r + 1) * 128] = 1.0
    c["sel_bf"] = sel.astype(ml_dtypes.bfloat16)
    return c


class MK:
    def __init__(self, debug_phase_a=False, dev_chunks=None, dev_full=False,
                 dev_route=False):
        self.debug_phase_a = debug_phase_a or dev_full
        self.dev_full = dev_full
        self.dev_route = dev_route
        self.dev_chunks = dev_chunks
        self.dumps = {}
        self.nc = bass.Bass("TRN2", target_bir_lowering=False)
        self.s = Sched()
        self.final = []
        self._declare_io()
        self._alloc_sbuf()
        self._alloc_psum()
        self.s.keymap = dict(self.key)

    def _declare_io(self):
        nc = self.nc
        D = lambda n, sh, dt, kind: nc.dram_tensor(n, sh, dt, kind=kind)
        self.x_in = D("x_in", [TOK, D_MODEL], F32, "ExternalInput")
        self.w = {}
        for name, sh in [
            ("W_in", [D_MODEL, 2 * D_INNER]), ("conv_w", [D_INNER, 1, D_CONV]),
            ("conv_b", [D_INNER]), ("W_x", [D_INNER, 2 * D_STATE + 1]),
            ("W_dt", [1, D_INNER]), ("b_dt", [D_INNER]),
            ("A_log", [D_INNER, D_STATE]), ("D_param", [D_INNER]),
            ("W_out", [D_INNER, D_MODEL]),
            ("router_w", [E, D_MODEL]), ("bias", [E]),
            ("gate_w", [E, D_MODEL, 1024]), ("up_w", [E, D_MODEL, 1024]),
            ("down_w", [E, 1024, D_MODEL]),
            ("sh_gate_w", [D_MODEL, 2048]), ("sh_up_w", [D_MODEL, 2048]),
            ("sh_down_w", [2048, D_MODEL]),
            ("ident_f32", [128, 128]), ("ones_row", [1, 128]),
            ("ltri16", [16, 16]), ("ltri16i", [16, 16]),
            ("iota_p", [128, 1]), ("iota16", [128, 16]), ("iota_nb", [1, NB]),
            ("io999", [128, 16]),
        ]:
            self.w[name] = D(name, sh, F32, "ExternalInput")
        self.w["ident_bf"] = D("ident_bf", [128, 128], BF16, "ExternalInput")
        self.w["sel_bf"] = D("sel_bf", [33, 33 * 128], BF16, "ExternalInput")
        self.out = D("out", [TOK, D_MODEL], F32, "ExternalOutput")

        sk = "ExternalOutput" if self.debug_phase_a else "Internal"
        if getattr(self, "dev_route", False):
            self.x1_hbm = D("x1_hbm", [TOK, D_MODEL], F32, "ExternalInput")
            self.x1ln_hbm = D("x1ln_hbm", [TOK, D_MODEL], BF16,
                              "ExternalInput")
        else:
            self.x1_hbm = D("x1_hbm", [TOK, D_MODEL], F32, sk)
            self.x1ln_hbm = D("x1ln_hbm", [TOK, D_MODEL], BF16, sk)
        self.xg_hbm = D("xg_hbm", [NSLOT, D_MODEL], BF16, sk)
        self.yg_hbm = D("yg_hbm", [NSLOT, D_MODEL], BF16, sk)
        self.win_bf = D("win_bf", [D_MODEL, 2 * D_INNER], BF16, "Internal")
        self.wout_bf = D("wout_bf", [D_INNER, D_MODEL], BF16, "Internal")
        self.gate_tab = D("gate_tab", [E * 128, 8 * 1024], BF16, "Internal")
        self.up_tab = D("up_tab", [E * 128, 8 * 1024], BF16, "Internal")
        self.down_tab = D("down_tab", [E * 128, 8 * 1024], BF16, "Internal")
        self.shg_bf = D("shg_bf", [D_MODEL, 2048], BF16, "Internal")
        self.shu_bf = D("shu_bf", [D_MODEL, 2048], BF16, "Internal")
        self.shd_bf = D("shd_bf", [2048, D_MODEL], BF16, "Internal")

    def _alloc_sbuf(self):
        nc = self.nc
        self.sb = {}
        self.key = {}

        def T(name, shape, dt, key=None):
            self.sb[name] = nc.alloc_sbuf_tensor("sb_" + name, shape, dt).ap()
            self.key[name] = key or name

        T("ident_f32", [128, 128], F32)
        T("ident_bf", [128, 128], BF16)
        T("ones_row", [1, 128], F32)
        T("ltri16", [16, 16], F32)
        T("ltri16i", [16, 16], F32)
        T("iota_p", [128, 1], F32)
        T("iota16", [128, 16], F32)
        T("iota_nb", [1, NB], F32)
        T("io999", [128, 16], F32)
        T("sel_bf", [33, 33 * 128], BF16)
        T("convw", [128, NC_I, D_CONV], F32)
        T("convb", [128, NC_I], F32)
        T("dparam", [128, NC_I], F32)
        T("wdt", [128, NC_I], F32)
        T("bdt", [128, NC_I], F32)
        T("aneg", [128, NC_I, D_STATE], F32)
        T("wx_lhsT", [128, NC_I, 33], BF16)
        T("rout_rhs", [128, NC_D, E], BF16)
        T("bias_e", [128, E], F32)
        T("brow", [1, E], F32)
        T("lnrow", [128, 16], F32)
        T("epsr", [128, 1], F32)
        T("oner", [128, 1], F32)
        T("state", [128, NC_I, D_STATE], F32)
        T("xdbl", [33, Q], BF16)
        T("xt", [128, 2, D_MODEL], F32)
        T("sqd", [128, D_MODEL], F32)
        T("win_m", [128, 2, NC_D, 128], BF16)
        T("wout_k", [128, 2, D_MODEL], BF16)
        T("ctx2", [128, NC_I, 3], BF16)
        T("dA", [128, 2, Q], F32)
        T("ub", [128, 2, Q], BF16)
        T("prod", [128, 2, Q], BF16)
        T("x1t", [128, 2, D_MODEL], F32)
        T("x1lt", [128, 2, D_MODEL], BF16)

        AR = 8240 + 8 * 8192 + 4096
        arena = nc.alloc_sbuf_tensor("sb_arena", [128, AR], BF16).ap()
        self.arena = arena
        off = [0]

        def V(name, shape, key, elems=None, at=None):
            n = elems if elems is not None else int(np.prod(shape[1:]))
            o = at if at is not None else off[0]
            ap = arena[:, o:o + n]
            if len(shape) == 3:
                ap = ap.rearrange("p (a b) -> p a b", a=shape[1])
            self.sb[name] = ap
            self.key[name] = key
            if at is None:
                off[0] = o + n
            return o

        o_ctx = V("ctx", [128, NC_I, Q + 3], "Rctx")
        V("cc", [128, D_STATE, Q], "Rctx", at=o_ctx)
        o_zg = V("zg", [128, NC_I, Q], "Rzg", at=8240)
        off[0] = 8240 + 8192
        o_ctv = V("ctv", [128, NC_I, Q], "Rctv")
        V("ybf", [128, NC_I, Q], "Rctv", at=o_ctv)
        V("xcp", [128, NC_I, Q], "Rxcp")
        V("varg", [128, NC_I, Q], "Rvarg")
        o_sg = V("sg", [128, NC_I, Q], "Rsg")
        V("szg", [128, NC_I, Q], "Rsg", at=o_sg)
        V("lsgx", [128, NC_I, Q], "Rlsgx")
        V("nbc", [128, D_STATE, Q], "Rnbc")
        o_hc = V("hc", [128, D_STATE, Q], "Rhc")
        self.sb["rtmp"] = arena[:, 0:2048].bitcast(F32)[64:80, :]
        self.key["rtmp"] = "Rctx"

        V("xln_tm", [128, NTT, D_MODEL], "Rhc", at=o_hc)
        V("xln_fm", [128, NC_D, Q], "Rxfm", elems=NC_D * Q)
        self.arena_used = off[0]

        for nm, pair in [("ctx", ("Rctx", "Rzg")), ("cc", ("Rctx",)),
                         ("lsgx", ("RlsgxA", "RlsgxB")),
                         ("nbc", ("RnbcA", "RnbcB")),
                         ("hc", ("RhcA", "RhcB")),
                         ("xln_tm", ("RhcA",))]:
            self.key[nm] = pair
        self.key["ctx"] = ("Rctx",)
        self.key["rtmp"] = ("Rctx",)

        def VF(name, shape, at, keys, dtype=BF16):
            n = int(np.prod(shape[1:]))
            if dtype == BF16:
                ap = arena[:, at:at + n]
            else:
                ap = arena[:, at:at + 2 * n].bitcast(F32)
            if shape[0] != 128:
                ap = ap[0:shape[0]]
            if len(shape) == 3:
                ap = ap.rearrange("p (a b) -> p a b", a=shape[1])
            elif len(shape) == 4:
                ap = ap.rearrange("p (a b c) -> p a b c", a=shape[1],
                                  b=shape[2])
            self.sb[name] = ap
            self.key[name] = keys

        O_ZG, O_CTV, O_XCP, O_VARG, O_SG, O_LSGX, O_NBC, O_HC, O_XFM = (
            8240, 16432, 24624, 32816, 41008, 49200, 57392, 65584, 73776)
        VF("flags", [16, TOK], 0, ("Rctx", "Rzg"), F32)
        VF("pos", [16, TOK], O_CTV, ("Rctv", "Rxcp"), F32)
        VF("ones16k", [16, TOK], O_VARG, ("Rvarg", "Rsg"), F32)
        VF("post", [128, 64, E], O_HC, ("RhcA",), F32)
        VF("rfm", [128, NC_D, 128], O_LSGX, ("RlsgxA",))
        VF("wg_t0", [128, 8 * 1024], 0, ("Rctx",))
        VF("wg_t1", [128, 8 * 1024], O_ZG, ("Rzg",))
        VF("wu_t0", [128, 8 * 1024], O_CTV, ("Rctv",))
        VF("wu_t1", [128, 8 * 1024], O_XCP, ("Rxcp",))
        VF("wd_t0", [128, 8 * 1024], O_VARG, ("Rvarg",))
        VF("wd_t1", [128, 8 * 1024], O_SG, ("Rsg",))
        VF("xg_fm0", [128, NC_D, FB], O_LSGX, ("RlsgxA",))
        VF("xg_fm1", [128, NC_D, FB], O_LSGX + 4096, ("RlsgxB",))
        VF("sgate", [128, NC_D, FB], O_NBC, ("RnbcA",))
        VF("hub", [128, NC_D, FB], O_NBC + 4096, ("RnbcB",))
        VF("ygb0", [128, NTT, D_MODEL], O_HC, ("RhcA",))
        VF("ygb1", [128, NTT, D_MODEL], O_HC + 4096, ("RhcB",))
        VF("shgw", [128, NC_D, 2048], 0, ("Rctx", "Rzg"))
        VF("shuw", [128, NC_D, 2048], O_CTV, ("Rctv", "Rxcp"))
        VF("shdw", [128, NC_I, 1024], O_VARG, ("Rvarg", "Rsg"))
        VF("sxfm", [128, NC_D, FB], O_LSGX, ("RlsgxA",))
        VF("tsil", [128, 1, FB], O_XFM, ("Rxfm",))
        VF("gat0", [128, TOPK, D_MODEL], O_NBC, ("RnbcA",))
        VF("gat1", [128, TOPK, D_MODEL], O_NBC + 4096, ("RnbcB",))
        VF("shhub", [128, NC_I, FB], O_HC, ("RhcA", "RhcB"))
        VF("eg", [128, 1, FB], O_XFM, ("Rxfm",), F32)
        VF("tmu", [128, 1, FB], O_XFM + 2048, ("Rxfm",), F32)
        VF("eg4", [128, 1, FB], O_LSGX + 4096, ("RlsgxB",), F32)
        xtf = self.sb["xt"].rearrange("p a b -> p (a b)")
        self.sb["cw"] = xtf[:, 0:256].rearrange("p (a b) -> p a b", a=64)
        self.sb["eki"] = xtf[:, 256:512].rearrange("p (a b) -> p a b", a=64)
        self.sb["dstf"] = xtf[:, 512:768].rearrange("p (a b) -> p a b", a=64)
        self.sb["dsti"] = xtf[:, 768:1024].bitcast(I32).rearrange(
            "p (a b) -> p a b", a=64)
        self.sb["scores"] = xtf[:, 1024:1040]
        self.sb["ot2"] = xtf[:, 1168:1680]
        self.key["ot2"] = ("xt0", "xt1")
        self.sb["srt"] = xtf[:, 1040:1168].rearrange(
            "p (a b) -> p a b", a=8)
        for nm in ["cw", "eki", "dstf", "dsti", "scores", "srt"]:
            self.key[nm] = ("xt0", "xt1")
        sq = self.sb["sqd"]
        self.sb["ot"] = sq[:, 0:1024]
        self.sb["alog"] = sq[:, 256:512].rearrange("p (c s) -> p c s", c=NC_I)
        self.sb["apos"] = sq[:, 512:768].rearrange("p (c s) -> p c s", c=NC_I)
        for nm in ["alog", "apos"]:
            self.key[nm] = "sqd"

        self.sb["eidr"] = sq[0:1, 0:NB]
        self.sb["cnt"] = sq[0:16, 256:264]
        self.sb["base_bc"] = sq[:, 320:336]
        self.sb["cumbr"] = sq[0:1, 384:400]
        self.sb["baser"] = sq[0:1, 448:464]
        self.sb["widx"] = sq.bitcast(I32)[:, 512:512 + NB]
        self.sb["ones16c"] = sq[0:16, 640:641]
        for nm in ["eidr", "cnt", "base_bc", "cumbr", "baser", "widx",
                   "ones16c"]:
            self.key[nm] = "sqd"
        self.sbuf_free_after_a = nc.sbuf_bytes_remaining

    def kn(self, *names):
        return [self.key.get(n, n) for n in names]

    def _alloc_psum(self):
        self.ps = {}
        for i in range(8):
            self.ps[i] = self.nc.alloc_psum_tensor(f"ps{i}", [128, 512],
                                                   F32).ap()

    def dma(self, q, out, in_, r=(), w=(), tag="", **kw):
        if not getattr(self, "_warm_done", None):
            self._warm_done = set()
        if q not in self._warm_done and hasattr(self, "_warm_gated"):
            self._warm_done.add(q)
            r = tuple(r) + ("warm",)
        return self.s.op(q, lambda e: e.dma_start(out=out, in_=in_, **kw),
                         r=r, w=w, dma=True, tag=tag)

    def act(self, fn, r=(), w=(), tag=""):
        return self.s.op("act", fn, r=r, w=w, tag=tag)

    def dve(self, fn, r=(), w=(), tag=""):
        return self.s.op("dve", fn, r=r, w=w, tag=tag)

    def pool(self, fn, r=(), w=(), tag="", dma=False):
        return self.s.op("pool", fn, r=r, w=w, tag=tag, dma=dma)

    def pe(self, fn, r=(), w=(), tag=""):
        return self.s.op("pe", fn, r=r, w=w, tag=tag)

    def tr_fenced(self, n, srcs, psel, outs, tag):
        for kc in range(n + 1):
            src = srcs(kc % n)
            self.pe(lambda e, kc=kc, src=src, psel=psel: e.matmul(
                out=psel(kc)[0], lhsT=src[0], rhs=src[1],
                start=True, stop=True),
                r=list(src[2]) + ([f"{tag}F{kc - 1}"] if kc else []),
                w=[psel(kc)[1], f"{tag}F{kc}"],
                tag=f"{tag}tr{kc}")
            if kc > 0:
                outs(kc - 1, psel(kc - 1)[0], psel(kc - 1)[1],
                     f"{tag}F{kc}")

    def dump_raw(self, name, ap, key, dtype):
        sh = list(ap.shape)
        t = self.nc.dram_tensor("dbg_" + name, sh, dtype,
                                kind="ExternalOutput")
        o = self.dma("sp", t.ap(), ap, r=[key], w=[f"dbg_{name}"],
                     tag=f"dump_{name}")
        self.final.append(o)

    def dump(self, name, ap, key):
        sh = list(ap.shape)
        t = self.nc.dram_tensor("dbg_" + name, sh, F32,
                                kind="ExternalOutput")
        o = self.dma("pool", t.ap(), ap, r=[key], w=[f"dbg_{name}"],
                     tag=f"dump_{name}")
        self.final.append(o)

    def build_init(self):
        sb, w, ps = self.sb, self.w, self.ps
        for i in range(3):
            self.act(lambda e: e.activation(
                out=sb["sqd"][:, 0:1024], in_=sb["sqd"][:, 0:1024],
                func=AF.Exp),
                r=["warm"], w=["warm", "sqd"], tag=f"warm{i}")
        self._warm_gated = False
        for name in ["ident_f32", "ident_bf", "ones_row", "ltri16", "ltri16i",
                     "iota_p", "iota16", "iota_nb", "sel_bf", "io999"]:
            self.dma("sp", sb[name], w[name].ap(), w=[name], tag=f"ld_{name}")
        self.dma("sp", sb["convw"],
                 w["conv_w"].ap().rearrange("(c p) one k -> p c (one k)",
                                            p=128),
                 w=["convw"], tag="ld_convw")
        for nm, src in [("convb", "conv_b"), ("dparam", "D_param"),
                        ("bdt", "b_dt")]:
            self.dma("sp", sb[nm],
                     w[src].ap().rearrange("(c p) -> p c", p=128),
                     w=[nm], tag=f"ld_{nm}", allow_slow_non_contiguous=True)
        self.dma("sp", sb["wdt"],
                 w["W_dt"].ap().rearrange("one (c p) -> p (one c)", p=128),
                 w=["wdt"], tag="ld_wdt", allow_slow_non_contiguous=True)
        self.dma("sp", sb["alog"],
                 w["A_log"].ap().rearrange("(c p) s -> p c s", p=128),
                 w=["alog"], tag="ld_alog")
        self.act(lambda e: e.activation(
            out=sb["apos"].rearrange("p c s -> p (c s)"),
            in_=sb["alog"].rearrange("p c s -> p (c s)"), func=AF.Exp),
            r=["alog"], w=["apos"], tag="exp_alog")
        self.dve(lambda e: e.tensor_scalar(
            out=sb["aneg"].rearrange("p c s -> p (c s)"),
            in0=sb["apos"].rearrange("p c s -> p (c s)"),
            scalar1=-1.0, scalar2=None, op0=ALU.mult),
            r=["apos"], w=["aneg"], tag="neg_alog")
        self.dma("pool", sb["wx_lhsT"],
                 w["W_x"].ap().rearrange("(c p) m -> p c m", p=128),
                 w=["wx_lhsT"], tag="ld_wx")
        self.dma("sp", sb["rtmp"], w["router_w"].ap(), w=["rtmp"],
                 tag="ld_router")
        self.tr_fenced(
            NC_D,
            lambda kc: (sb["rtmp"][:, kc * 128:(kc + 1) * 128],
                        sb["ident_f32"][64:80, 64:80], ["rtmp", "ident_f32"]),
            lambda kc: (ps[kc % 2][:, 0:16], f"ps{kc % 2}"),
            lambda kc, psap, pskey, fence: self.act(
                lambda e, kc=kc, psap=psap: e.activation(
                    out=sb["rout_rhs"][:, kc, :], in_=psap,
                    func=AF.Identity),
                r=[pskey, fence], w=["rout_rhs"], tag=f"ev_rout{kc}"),
            "Rt")
        self.dma("sp", sb["brow"], w["bias"].ap().unsqueeze(0), w=["brow"],
                 tag="ld_bias")
        self.pe(lambda e: e.matmul(out=ps[1][:, 0:E], lhsT=sb["ones_row"],
                                   rhs=sb["brow"], start=True, stop=True),
                r=["brow", "ones_row"], w=["ps1"], tag="bc_bias")
        self.act(lambda e: e.activation(out=sb["bias_e"], in_=ps[1][:, 0:E],
                                        func=AF.Identity),
                 r=["ps1"], w=["bias_e"], tag="ev_bias")
        self.dma("pool", self.win_bf.ap(), w["W_in"].ap(), w=["win_bf"],
                 tag="pre_win")
        self.dma("pool", self.wout_bf.ap(), w["W_out"].ap(), w=["wout_bf"],
                 tag="pre_wout")
        for tab, src in [("gate_tab", "gate_w"), ("up_tab", "up_w"),
                         ("down_tab", "down_w")]:
            tb = getattr(self, tab).ap().rearrange(
                "(e p) (kc m) -> e p kc m", p=128, kc=8)
            sr = w[src].ap().rearrange("e (kc p) m -> e p kc m", p=128)
            for ei in range(E):
                self.dma("pool", tb[ei], sr[ei], w=[tab],
                         tag=f"pre_{tab}{ei}")
        self.dma("pool", self.shg_bf.ap(), w["sh_gate_w"].ap(), w=["shg_bf"],
                 tag="pre_shg")
        self.dma("pool", self.shu_bf.ap(), w["sh_up_w"].ap(), w=["shu_bf"],
                 tag="pre_shu")
        self.dma("pool", self.shd_bf.ap(), w["sh_down_w"].ap(), w=["shd_bf"],
                 tag="pre_shd")
        self.dve(lambda e: e.memset(sb["ybf"], 0.0), w=["ybf"], tag="zt")
        xg = self.xg_hbm.ap().rearrange("(r p) d -> p r d", p=128)
        nrow = NSLOT // 128
        step = 8
        for i in range(0, nrow, step):
            n = min(step, nrow - i)
            self.dma("sp", xg[:, i:i + n, :],
                     sb["ybf"].rearrange("p c q -> p (c q)")[:, 0:n * D_MODEL]
                     .rearrange("p (n d) -> p n d", n=n),
                     r=["ybf"], w=["xg_hbm"], tag=f"zxg{i}")
        self.dve(lambda e: e.memset(sb["state"], 0.0), w=["state"], tag="zst")
        self.dve(lambda e: e.memset(sb["epsr"], LN_EPS), w=["epsr"],
                 tag="ldeps")
        self.dve(lambda e: e.memset(sb["oner"], 1.0), w=["oner"],
                 tag="ldone")

    def _ln_tile(self, src_ap, dst_ap, rkeys, wkey, tag):
        sb = self.sb
        ln = sb["lnrow"]
        self.dve(lambda e: e.reduce_sum(out=ln[:, 0:1], in_=src_ap, axis=AX),
                 r=rkeys, w=["lnrow"], tag=f"{tag}_s")
        self.act(lambda e: e.activation(out=sb["sqd"], in_=src_ap,
                                        func=AF.Square,
                                        accum_out=ln[:, 1:2]),
                 r=rkeys, w=["lnrow", "sqd"], tag=f"{tag}_q")
        self.dve(lambda e: e.tensor_scalar(
            out=ln[:, 2:3], in0=ln[:, 0:1], scalar1=-1.0 / D_MODEL,
            scalar2=None, op0=ALU.mult),
            r=["lnrow"], w=["lnrow"], tag=f"{tag}_m")
        self.dve(lambda e: e.tensor_tensor(
            out=ln[:, 3:4], in0=ln[:, 2:3], in1=ln[:, 2:3], op=ALU.mult),
            r=["lnrow"], w=["lnrow"], tag=f"{tag}_m2")
        self.dve(lambda e: e.tensor_scalar(
            out=ln[:, 4:5], in0=ln[:, 1:2], scalar1=1.0 / D_MODEL,
            scalar2=None, op0=ALU.mult),
            r=["lnrow"], w=["lnrow"], tag=f"{tag}_v1")
        self.dve(lambda e: e.tensor_tensor(
            out=ln[:, 5:6], in0=ln[:, 4:5], in1=ln[:, 3:4], op=ALU.subtract),
            r=["lnrow"], w=["lnrow"], tag=f"{tag}_v")
        self.act(lambda e: e.activation(out=ln[:, 6:7], in_=ln[:, 5:6],
                                        func=AF.Ln, bias=sb["epsr"]),
                 r=["lnrow", "epsr"], w=["lnrow"], tag=f"{tag}_l")
        self.act(lambda e: e.activation(out=ln[:, 7:8], in_=ln[:, 6:7],
                                        func=AF.Exp, scale=-0.5),
                 r=["lnrow"], w=["lnrow"], tag=f"{tag}_r")
        self.dve(lambda e: e.tensor_tensor(
            out=sb["sqd"][:, 0:D_MODEL], in0=src_ap,
            in1=ln[:, 2:3].broadcast_to([128, D_MODEL]), op=ALU.add),
            r=list(rkeys) + ["lnrow"], w=["sqd"], tag=f"{tag}_a1")
        self.dve(lambda e: e.tensor_tensor(
            out=dst_ap, in0=sb["sqd"][:, 0:D_MODEL],
            in1=ln[:, 7:8].broadcast_to([128, D_MODEL]), op=ALU.mult),
            r=["sqd", "lnrow"], w=[wkey], tag=f"{tag}_a2")

    def build_phase_a(self, seq):
        sb = self.sb
        if seq > 0:
            self.dve(lambda e: e.memset(sb["state"], 0.0), w=["state"],
                     tag=f"zs{seq}")
        nch = self.dev_chunks or NCHUNK
        for k in range(nch):
            t0 = seq * SEQ + k * Q
            tag = f"s{seq}c{k}_"
            self._chunk_ln_win(t0, k, tag)
            self._chunk_conv_dt(k, tag)
            self._chunk_scan(k, tag)
            self._chunk_wout(t0, k, tag)

    def _chunk_ln_win(self, t0, k, tag):
        sb, ps = self.sb, self.ps
        x = self.x_in.ap()
        for tt in range(NTT):
            b = tt % 2
            xt = sb["xt"][:, b, :]
            self.dma("sp", xt, x[t0 + tt * 128:t0 + (tt + 1) * 128, :],
                     w=[f"xt{b}"], tag=f"{tag}ldx{tt}")
            self._ln_tile(xt, sb["xln_tm"][:, tt, :], [f"xt{b}"], "xln_tm",
                          f"{tag}ln{tt}")
            for kc in range(NC_D):
                self.dma("sp", sb["xln_fm"][:, kc, tt * 128:(tt + 1) * 128],
                         sb["xln_tm"][:, tt, kc * 128:(kc + 1) * 128],
                         r=["xln_tm"], w=["xln_fm"], transpose=True,
                         tag=f"{tag}trx{tt}_{kc}")
        if self.dev_chunks and tag == "s0c0_":
            self.dump("xln_tm0", sb["xln_tm"], "Rhc")
        winap = self.win_bf.ap()
        for m in range(32):
            b = m % 2
            self.dma("sp", sb["win_m"][:, b],
                     winap[:, m * 128:(m + 1) * 128]
                     .rearrange("(kc p) m -> p kc m", p=128),
                     r=["win_bf"], w=[f"win{b}"], tag=f"{tag}ldw{m}")
            pk = 2 + (m % 2)
            for kc in range(NC_D):
                self.pe(lambda e, m=m, kc=kc, b=b, pk=pk: e.matmul(
                    out=ps[pk][:, 0:Q], lhsT=sb["win_m"][:, b, kc, :],
                    rhs=sb["xln_fm"][:, kc, :], start=(kc == 0),
                    stop=(kc == NC_D - 1)),
                    r=[f"win{b}", "xln_fm"], w=[f"ps{pk}"],
                    tag=f"{tag}mwin{m}_{kc}")
            if m < 16:
                dst, wkey = sb["ctx"][:, m, 3:3 + Q], "ctx"
            else:
                dst, wkey = sb["zg"][:, m - 16, :], "zg"
            self.act(lambda e, dst=dst, pk=pk: e.activation(
                out=dst, in_=ps[pk][:, 0:Q], func=AF.Identity),
                r=[f"ps{pk}"], w=[wkey], tag=f"{tag}evw{m}")

    def _chunk_conv_dt(self, k, tag):
        sb, ps = self.sb, self.ps
        if k == 0:
            self.dve(lambda e: e.memset(sb["ctx"][:, :, 0:3], 0.0),
                     w=["ctx"], tag=f"{tag}cpad")
        else:
            self.dve(lambda e: e.tensor_copy(out=sb["ctx"][:, :, 0:3],
                                             in_=sb["ctx2"]),
                     r=["ctx2"], w=["ctx"], tag=f"{tag}ctail")
        for c in range(NC_I):
            bufs = [sb["ctv"], sb["xcp"]]
            keys = ["ctv", "xcp"]
            self.dve(lambda e, c=c: e.tensor_scalar(
                out=sb["ctv"][:, c, :], in0=sb["ctx"][:, c, 0:Q],
                scalar1=sb["convw"][:, c, 0:1], scalar2=None, op0=ALU.mult),
                r=["ctx", "convw"], w=["ctv"], tag=f"{tag}cv{c}_0")
            for kk in range(1, D_CONV):
                src = bufs[(kk - 1) % 2]
                dst = bufs[kk % 2]
                self.dve(lambda e, c=c, kk=kk, src=src, dst=dst:
                         e.scalar_tensor_tensor(
                    out=dst[:, c, :], in0=sb["ctx"][:, c, kk:kk + Q],
                    scalar=sb["convw"][:, c, kk:kk + 1],
                    in1=src[:, c, :], op0=ALU.mult, op1=ALU.add),
                    r=["ctx", "convw", keys[(kk - 1) % 2]],
                    w=[keys[kk % 2]], tag=f"{tag}cv{c}_{kk}")
        self.dve(lambda e: e.tensor_copy(out=sb["ctx2"],
                                         in_=sb["ctx"][:, :, Q:Q + 3]),
                 r=["ctx"], w=["ctx2"], tag=f"{tag}ctsv")
        if self.dev_chunks and tag == "s0c0_":
            self.dump("xc0", sb["ctx"], "Rctx")
        for c in range(NC_I):
            self.dve(lambda e, c=c: e.tensor_scalar(
                out=sb["sg"][:, c, :], in0=sb["xcp"][:, c, :],
                scalar1=sb["convb"][:, c:c + 1], scalar2=None, op0=ALU.add),
                r=["xcp", "convb"], w=["sg"], tag=f"{tag}vpb{c}")
        sgf = sb["sg"].rearrange("p c q -> p (c q)")
        ctf = sb["ctv"].rearrange("p c q -> p (c q)")
        xcf = sb["xcp"].rearrange("p c q -> p (c q)")
        self.act(lambda e: e.activation(out=ctf, in_=sgf, func=AF.Exp,
                                        scale=-1.0),
                 r=["sg"], w=["ctv"], tag=f"{tag}sl_e")
        self.act(lambda e: e.activation(out=xcf, in_=ctf, func=AF.Ln,
                                        bias=sb["oner"]),
                 r=["ctv", "oner"], w=["xcp"], tag=f"{tag}sl_l")
        self.act(lambda e: e.activation(out=ctf, in_=xcf, func=AF.Exp,
                                        scale=-1.0),
                 r=["xcp"], w=["ctv"], tag=f"{tag}sl_s")
        self.dve(lambda e: e.tensor_tensor(out=xcf, in0=sgf, in1=ctf,
                                           op=ALU.mult),
                 r=["sg", "ctv"], w=["xcp"], tag=f"{tag}sl_m")
        for c in range(NC_I):
            self.pe(lambda e, c=c: e.matmul(
                out=ps[4][0:33, 0:Q], lhsT=sb["wx_lhsT"][:, c, :],
                rhs=sb["xcp"][:, c, :], start=(c == 0), stop=(c == NC_I - 1)),
                r=["wx_lhsT", "xcp"], w=["ps4"], tag=f"{tag}mwx{c}")
        self.act(lambda e: e.activation(out=sb["xdbl"], in_=ps[4][0:33, 0:Q],
                                        func=AF.Identity),
                 r=["ps4"], w=["xdbl"], tag=f"{tag}evx")
        self.pe(lambda e: e.matmul(out=ps[5][:, 0:Q],
                                   lhsT=sb["sel_bf"][:, 0:128],
                                   rhs=sb["xdbl"], start=True, stop=True),
                r=["xdbl", "sel_bf"], w=["ps5"], tag=f"{tag}bdt")
        for c in range(NC_I):
            self.act(lambda e, c=c: e.activation(
                out=sb["varg"][:, c, :], in_=ps[5][:, 0:Q], func=AF.Identity,
                scale=sb["wdt"][:, c:c + 1], bias=sb["bdt"][:, c:c + 1]),
                r=["ps5", "wdt", "bdt"], w=["varg"], tag=f"{tag}va{c}")
        vaf = sb["varg"].rearrange("p c q -> p (c q)")
        sgf2 = sb["sg"].rearrange("p c q -> p (c q)")
        self.act(lambda e: e.activation(out=sgf2, in_=vaf, func=AF.Exp),
                 r=["varg"], w=["sg"], tag=f"{tag}spe")
        self.act(lambda e: e.activation(out=vaf, in_=sgf2, func=AF.Ln,
                                        bias=sb["oner"]),
                 r=["sg", "oner"], w=["varg"], tag=f"{tag}spl")
        self.dve(lambda e: e.tensor_tensor(
            out=sb["lsgx"].rearrange("p c q -> p (c q)"),
            in0=vaf,
            in1=sb["xcp"].rearrange("p c q -> p (c q)"), op=ALU.mult),
            r=["varg", "xcp"], w=["lsgx"], tag=f"{tag}lx")
        zf = sb["zg"].rearrange("p c q -> p (c q)")
        szf = sb["szg"].rearrange("p c q -> p (c q)")
        ctf2 = sb["ctv"].rearrange("p c q -> p (c q)")
        self.act(lambda e: e.activation(out=ctf2, in_=zf, func=AF.Exp,
                                        scale=-1.0),
                 r=["zg"], w=["ctv"], tag=f"{tag}sz_e")
        self.act(lambda e: e.activation(out=szf, in_=ctf2, func=AF.Ln,
                                        bias=sb["oner"]),
                 r=["ctv", "oner"], w=["szg"], tag=f"{tag}sz_l")
        self.act(lambda e: e.activation(out=ctf2, in_=szf, func=AF.Exp,
                                        scale=-1.0),
                 r=["szg"], w=["ctv"], tag=f"{tag}sz_s")
        self.dve(lambda e: e.tensor_tensor(out=szf, in0=zf, in1=ctf2,
                                           op=ALU.mult),
                 r=["zg", "ctv"], w=["szg"], tag=f"{tag}sz_m")
        for s in range(D_STATE):
            self.pe(lambda e, s=s: e.matmul(
                out=ps[6][:, 0:Q],
                lhsT=sb["sel_bf"][:, (1 + s) * 128:(2 + s) * 128],
                rhs=sb["xdbl"], start=True, stop=True),
                r=["xdbl", "sel_bf"], w=["ps6"], tag=f"{tag}bB{s}")
            self.act(lambda e, s=s: e.activation(
                out=sb["nbc"][:, s, :], in_=ps[6][:, 0:Q], func=AF.Identity),
                r=["ps6"], w=["nbc"], tag=f"{tag}eB{s}")
            self.pe(lambda e, s=s: e.matmul(
                out=ps[7][:, 0:Q],
                lhsT=sb["sel_bf"][:, (17 + s) * 128:(18 + s) * 128],
                rhs=sb["xdbl"], start=True, stop=True),
                r=["xdbl", "sel_bf"], w=["ps7"], tag=f"{tag}bC{s}")
            self.act(lambda e, s=s: e.activation(
                out=sb["cc"][:, s, :], in_=ps[7][:, 0:Q], func=AF.Identity),
                r=["ps7"], w=["cc"], tag=f"{tag}eC{s}")

    def _chunk_scan(self, k, tag):
        sb, ps = self.sb, self.ps
        for c in range(NC_I):
            pk = c % 2
            for s in range(D_STATE):
                db = s % 2
                self.act(lambda e, c=c, s=s, db=db: e.activation(
                    out=sb["dA"][:, db, :], in_=sb["varg"][:, c, :],
                    func=AF.Exp, scale=sb["aneg"][:, c, s:s + 1]),
                    r=["varg", "aneg"], w=[f"dA{db}"], tag=f"{tag}dA{c}_{s}")
                self.dve(lambda e, c=c, s=s, db=db: e.tensor_tensor(
                    out=sb["ub"][:, db, :], in0=sb["lsgx"][:, c, :],
                    in1=sb["nbc"][:, s, :], op=ALU.mult),
                    r=["lsgx", "nbc"], w=[f"ub{db}"], tag=f"{tag}u{c}_{s}")
                self.dve(lambda e, c=c, s=s, db=db:
                         e.tensor_tensor_scan(
                    out=sb["hc"][:, s, :], data0=sb["dA"][:, db, :],
                    data1=sb["ub"][:, db, :],
                    initial=sb["state"][:, c, s:s + 1],
                    op0=ALU.mult, op1=ALU.add),
                    r=[f"dA{db}", f"ub{db}", "state"], w=["hc"],
                    tag=f"{tag}sc{c}_{s}")
                self.dve(lambda e, s=s, db=db: e.tensor_tensor(
                    out=sb["prod"][:, db, :], in0=sb["hc"][:, s, :],
                    in1=sb["cc"][:, s, :], op=ALU.mult),
                    r=["hc", "cc"], w=[f"prod{db}"],
                    tag=f"{tag}cm{c}_{s}")
                self.pe(lambda e, s=s, db=db, pk=pk: e.matmul(
                    out=ps[pk][:, 0:Q], lhsT=sb["ident_bf"],
                    rhs=sb["prod"][:, db, :], start=(s == 0),
                    stop=(s == D_STATE - 1)),
                    r=[f"prod{db}", "ident_bf"], w=[f"ps{pk}"],
                    tag=f"{tag}ac{c}_{s}")
                if self.dev_chunks and c == NC_I - 1 and tag == "s0c0_":
                    dsn = sb["wg_t0"].rearrange("p (a b) -> p a b", a=16)
                    usn = sb["wg_t1"].rearrange("p (a b) -> p a b", a=16)
                    self.dve(lambda e, s=s, db=db, dsn=dsn: e.tensor_copy(
                        out=dsn[:, s, :], in_=sb["dA"][:, db, :]),
                        r=[f"dA{db}"], w=["wg_t0"], tag=f"{tag}snd{s}")
                    self.dve(lambda e, s=s, db=db, usn=usn: e.tensor_copy(
                        out=usn[:, s, :], in_=sb["ub"][:, db, :]),
                        r=[f"ub{db}"], w=["wg_t1"], tag=f"{tag}snu{s}")
            self.dve(lambda e, c=c: e.tensor_copy(
                out=sb["state"][:, c, :],
                in_=sb["hc"][:, :, Q - 1:Q].squeeze(2)),
                r=["hc"], w=["state"], tag=f"{tag}st{c}")
            self.dve(lambda e, c=c, pk=pk: e.scalar_tensor_tensor(
                out=sb["prod"][:, 0, :], in0=sb["xcp"][:, c, :],
                scalar=sb["dparam"][:, c:c + 1], in1=ps[pk][:, 0:Q],
                op0=ALU.mult, op1=ALU.add),
                r=["xcp", "dparam", f"ps{pk}"], w=["prod0"],
                tag=f"{tag}g1_{c}")
            self.dve(lambda e, c=c: e.tensor_tensor(
                out=sb["ybf"][:, c, :], in0=sb["prod"][:, 0, :],
                in1=sb["szg"][:, c, :], op=ALU.mult),
                r=["prod0", "szg"], w=["ybf"], tag=f"{tag}g2_{c}")

    def _chunk_wout(self, t0, k, tag):
        sb, ps = self.sb, self.ps
        woap = self.wout_bf.ap()
        for kc in range(NC_I):
            b = kc % 2
            self.dma("sp", sb["wout_k"][:, b, :],
                     woap[kc * 128:(kc + 1) * 128, :],
                     r=["wout_bf"], w=[f"wo{b}"], tag=f"{tag}lwo{kc}")
            for tt in range(NTT):
                for n in range(2):
                    pk = tt * 2 + n
                    self.pe(lambda e, kc=kc, tt=tt, n=n, b=b, pk=pk: e.matmul(
                        out=ps[pk][:, 0:512],
                        lhsT=sb["ybf"][:, kc, tt * 128:(tt + 1) * 128],
                        rhs=sb["wout_k"][:, b, n * 512:(n + 1) * 512],
                        start=(kc == 0), stop=(kc == NC_I - 1)),
                        r=[f"wo{b}", "ybf"], w=[f"ps{pk}"],
                        tag=f"{tag}mo{kc}_{tt}_{n}")
        x = self.x_in.ap()
        for tt in range(NTT):
            b = tt % 2
            xt = sb["xt"][:, b, :]
            rowsl = slice(t0 + tt * 128, t0 + (tt + 1) * 128)
            self.dma("sp", xt, x[rowsl, :], w=[f"xt{b}"], tag=f"{tag}rlx{tt}")
            x1 = sb["x1t"][:, b, :]
            for n in range(2):
                pk = tt * 2 + n
                self.dve(lambda e, xt=xt, x1=x1, n=n, pk=pk: e.tensor_tensor(
                    out=x1[:, n * 512:(n + 1) * 512],
                    in0=ps[pk][:, 0:512],
                    in1=xt[:, n * 512:(n + 1) * 512], op=ALU.add),
                    r=[f"ps{pk}", f"xt{b}"], w=[f"x1t{b}"],
                    tag=f"{tag}res{tt}_{n}")
            self.dma("sp", self.x1_hbm.ap()[rowsl, :], x1, r=[f"x1t{b}"],
                     w=["x1_hbm"], tag=f"{tag}sx1_{tt}")
            x1l = sb["x1lt"][:, b, :]
            self._ln_tile(x1, x1l, [f"x1t{b}"], f"x1lt{b}", f"{tag}ln2_{tt}")
            self.dma("sp", self.x1ln_hbm.ap()[rowsl, :], x1l,
                     r=[f"x1lt{b}"], w=["x1ln_hbm"], tag=f"{tag}sxl_{tt}")

    def build_ffn(self):
        import os
        stages = int(os.environ.get("FFN_STAGES", "5"))
        sb = self.sb
        dev = self.debug_phase_a and self.dev_full
        self._ffn_routing()
        if stages < 2:
            if dev:
                self.dump_raw("cw", sb["cw"], "xt0", F32)
                self.dump_raw("eki", sb["eki"], "xt0", F32)
            return
        if dev:
            self.dump_raw("cw", sb["cw"], "xt0", F32)
            self.dump_raw("eki", sb["eki"], "xt0", F32)
            self.dump_raw("biase", sb["bias_e"], "bias_e", F32)
            self.dump_raw("iota16d", sb["iota16"], "iota16", F32)
            self.dump_raw("io999d", sb["io999"], "io999", F32)
            self.dump_raw("srtf", sb["srt"], "xt0", F32)
            self.dump_raw("routr", sb["rout_rhs"], "rout_rhs", BF16)
        self._ffn_schedule()
        if stages < 3:
            return
        if dev:
            self.dump_raw("flags", sb["flags"], "Rctx", F32)
            self.dump_raw("posq", sb["pos"][:, 0:512], "Rctv", F32)
            self.dump_raw("cnt", sb["cnt"], "sqd", F32)
            self.dump_raw("eidr", sb["eidr"], "sqd", F32)
            self.dump_raw("widx", sb["widx"], "sqd", I32)
        self._ffn_scatter()
        if stages < 4:
            return
        if dev:
            self.dump_raw("dsti", sb["dsti"], "xt0", I32)
        self._ffn_blocks()
        if stages < 5:
            return
        self._ffn_shared_combine()

    def _ffn_routing(self):
        sb = self.sb
        dev = self.debug_phase_a and self.dev_full
        if dev:
            self.dbg_scall = self.nc.dram_tensor(
                "dbg_scall", [TOK // 128, 128, E], F32, kind="ExternalOutput")
        NTOK_T = TOK // 128
        for tt in range(NTOK_T):
            b = tt % 2
            xl = sb["x1lt"][:, b, :]
            self.dma("sp", xl, self.x1ln_hbm.ap()[tt * 128:(tt + 1) * 128, :],
                     r=["x1ln_hbm"], w=[f"x1lt{b}"], tag=f"r{tt}_ld")
            self._ffn_routing_tile(tt)

    def _ffn_routing_tile(self, tt):
        sb, ps = self.sb, self.ps
        b = tt % 2
        tg = f"r{tt}_"
        xl = sb["x1lt"][:, b, :]
        self.tr_fenced(
            NC_D,
            lambda kc: (xl[:, kc * 128:(kc + 1) * 128], sb["ident_bf"],
                        [f"x1lt{b}", "ident_bf"]),
            lambda kc: (ps[2 + kc % 2][:, 0:128], f"ps{2 + kc % 2}"),
            lambda kc, psap, pskey, fence: self.act(
                lambda e, kc=kc, psap=psap: e.activation(
                    out=sb["rfm"][:, kc, :], in_=psap, func=AF.Identity),
                r=[pskey, fence], w=["rfm"], tag=f"{tg}te{kc}"),
            f"{tg}RF")
        for kc in range(NC_D):
            self.pe(lambda e, kc=kc: e.matmul(
                out=ps[0][:, 0:E], lhsT=sb["rfm"][:, kc, :],
                rhs=sb["rout_rhs"][:, kc, :], start=(kc == 0),
                stop=(kc == NC_D - 1)),
                r=["rfm", "rout_rhs"], w=["ps0"], tag=f"{tg}mm{kc}")
        self.act(lambda e: e.activation(out=sb["scores"],
                                        in_=ps[0][:, 0:E],
                                        func=AF.Exp, scale=-1.0),
                 r=["ps0"], w=["scores"], tag=f"{tg}sg1")
        self.act(lambda e: e.activation(out=sb["srt"][:, 7, :],
                                        in_=sb["scores"],
                                        func=AF.Ln, bias=sb["oner"]),
                 r=["scores", "oner"], w=["srt"], tag=f"{tg}sg2")
        self.act(lambda e: e.activation(out=sb["scores"],
                                        in_=sb["srt"][:, 7, :],
                                        func=AF.Exp, scale=-1.0),
                 r=["srt"], w=["scores"], tag=f"{tg}sg3")
        self._routing_math(tt)

    def _routing_math(self, tt):
        sb, ps = self.sb, self.ps
        tg = f"r{tt}_"
        S = sb["srt"]
        d = self.dve
        d(lambda e: e.tensor_tensor(out=S[:, 0, :], in0=sb["scores"],
                                    in1=sb["bias_e"], op=ALU.add),
          r=["scores", "bias_e"], w=["srt"], tag=f"{tg}s")
        sgv = S[:, 0, :].rearrange("p (g e) -> p g e", g=NG)

        def tt2(o, a, bb, op, rk=("srt",), tag2=""):
            d(lambda e, o=o, a=a, bb=bb, op=op: e.tensor_tensor(
                out=o, in0=a, in1=bb, op=op),
              r=list(rk), w=["srt"], tag=f"{tg}{tag2}")
        v = [sgv[:, :, j] for j in range(4)]
        tt2(S[:, 1, 0:4], v[0], v[1], ALU.max, tag2="m1")
        tt2(S[:, 1, 4:8], v[0], v[1], ALU.min, tag2="n1")
        tt2(S[:, 1, 8:12], v[2], v[3], ALU.max, tag2="m2")
        tt2(S[:, 1, 12:16], v[2], v[3], ALU.min, tag2="n2")
        tt2(S[:, 2, 0:4], S[:, 1, 0:4], S[:, 1, 8:12], ALU.max, tag2="t1")
        tt2(S[:, 2, 4:8], S[:, 1, 0:4], S[:, 1, 8:12], ALU.min, tag2="ta")
        tt2(S[:, 2, 8:12], S[:, 1, 4:8], S[:, 1, 12:16], ALU.max,
            tag2="tb")
        tt2(S[:, 2, 12:16], S[:, 2, 4:8], S[:, 2, 8:12], ALU.max,
            tag2="sec")
        tt2(S[:, 3, 0:4], S[:, 2, 0:4], S[:, 2, 12:16], ALU.add,
            tag2="gsc")
        g = [S[:, 3, j:j + 1] for j in range(4)]
        tt2(S[:, 3, 4:5], g[0], g[1], ALU.max, tag2="M1")
        tt2(S[:, 3, 5:6], g[0], g[1], ALU.min, tag2="N1")
        tt2(S[:, 3, 6:7], g[2], g[3], ALU.max, tag2="M2")
        tt2(S[:, 3, 7:8], g[2], g[3], ALU.min, tag2="N2")
        tt2(S[:, 3, 8:9], S[:, 3, 4:5], S[:, 3, 6:7], ALU.min, tag2="ga")
        tt2(S[:, 3, 9:10], S[:, 3, 5:6], S[:, 3, 7:8], ALU.max, tag2="gb")
        tt2(S[:, 3, 10:11], S[:, 3, 8:9], S[:, 3, 9:10], ALU.max,
            tag2="thr")
        d(lambda e: e.tensor_tensor(out=S[:, 4, 0:4], in0=S[:, 3, 0:4],
                                    in1=S[:, 3, 10:11].broadcast_to([128, 4]),
                                    op=ALU.is_ge),
          r=["srt"], w=["srt"], tag=f"{tg}gm")
        gmbc = S[:, 4, 0:4].unsqueeze(2).broadcast_to([128, 4, 4])
        d(lambda e: e.tensor_scalar(out=S[:, 5, :], in0=S[:, 0, :],
                                    scalar1=1.0, scalar2=None,
                                    op0=ALU.add),
          r=["srt"], w=["srt"], tag=f"{tg}sp1")
        d(lambda e: e.tensor_tensor(
            out=S[:, 1, :].rearrange("p (g e) -> p g e", g=NG),
            in0=gmbc,
            in1=S[:, 5, :].rearrange("p (g e) -> p g e", g=NG),
            op=ALU.mult),
          r=["srt"], w=["srt"], tag=f"{tg}em")
        d(lambda e: e.tensor_scalar(out=S[:, 1, :], in0=S[:, 1, :],
                                    scalar1=-1.0, scalar2=None,
                                    op0=ALU.add),
          r=["srt"], w=["srt"], tag=f"{tg}mk")
        mrow = [1, 3]
        frow = [6, 0]
        for j in range(TOPK):
            mi = S[:, mrow[j % 2], :]
            mo = S[:, mrow[(j + 1) % 2], :]
            d(lambda e, mi=mi: e.tensor_reduce(out=S[:, 2, 0:1], in_=mi,
                                        axis=AX, op=ALU.max),
              r=["srt"], w=["srt"], tag=f"{tg}mx{j}")
            d(lambda e, mi=mi: e.tensor_tensor(out=S[:, 5, :], in0=mi,
                                        in1=S[:, 2, 0:1]
                                        .broadcast_to([128, 16]),
                                        op=ALU.is_equal),
              r=["srt"], w=["srt"], tag=f"{tg}eq{j}")
            d(lambda e: e.tensor_tensor(out=S[:, 4, :], in0=S[:, 5, :],
                                        in1=sb["io999"], op=ALU.mult),
              r=["srt", "io999"], w=["srt"], tag=f"{tg}cn{j}")
            d(lambda e: e.tensor_scalar(out=S[:, 7, :], in0=S[:, 4, :],
                                        scalar1=-1.0, scalar2=999.0,
                                        op0=ALU.mult, op1=ALU.add),
              r=["srt"], w=["srt"], tag=f"{tg}cf{j}")
            d(lambda e, j=j: e.tensor_reduce(out=sb["eki"][:, tt, j:j + 1],
                                             in_=S[:, 7, :], axis=AX,
                                             op=ALU.min),
              r=["srt"], w=["eki"], tag=f"{tg}ix{j}")
            d(lambda e, j=j: e.tensor_tensor(
                out=S[:, 5, :], in0=sb["iota16"],
                in1=sb["eki"][:, tt, j:j + 1].broadcast_to([128, 16]),
                op=ALU.is_equal),
              r=["eki", "iota16"], w=["srt"], tag=f"{tg}sl{j}")
            d(lambda e: e.tensor_tensor(out=S[:, 4, :], in0=S[:, 5, :],
                                        in1=sb["scores"], op=ALU.mult),
              r=["srt", "scores"], w=["srt"], tag=f"{tg}tw{j}")
            d(lambda e, j=j: e.tensor_reduce(out=S[:, 2, 2 + j:3 + j],
                                             in_=S[:, 4, :], axis=AX,
                                             op=ALU.add),
              r=["srt"], w=["srt"], tag=f"{tg}ts{j}")
            d(lambda e, mi=mi, mo=mo: e.scalar_tensor_tensor(
                out=mo, in0=S[:, 5, :], scalar=-1e9,
                in1=mi, op0=ALU.mult, op1=ALU.add),
              r=["srt"], w=["srt"], tag=f"{tg}up{j}")
            if j == 0:
                d(lambda e: e.tensor_copy(out=S[:, 6, :], in_=S[:, 5, :]),
                  r=["srt"], w=["srt"], tag=f"{tg}f0")
            else:
                fi = S[:, frow[(j + 1) % 2], :]
                fo = S[:, frow[j % 2], :]
                d(lambda e, fi=fi, fo=fo: e.tensor_tensor(out=fo,
                                            in0=fi,
                                            in1=S[:, 5, :], op=ALU.add),
                  r=["srt"], w=["srt"], tag=f"{tg}fa{j}")
        d(lambda e: e.tensor_reduce(out=S[:, 2, 6:7], in_=S[:, 2, 2:6],
                                    axis=AX, op=ALU.add),
          r=["srt"], w=["srt"], tag=f"{tg}tsm")
        d(lambda e: e.tensor_scalar(out=S[:, 2, 7:8], in0=S[:, 2, 6:7],
                                    scalar1=1e-20, scalar2=None,
                                    op0=ALU.add),
          r=["srt"], w=["srt"], tag=f"{tg}te")
        d(lambda e: e.reciprocal(out=S[:, 3, 0:1], in_=S[:, 2, 7:8]),
          r=["srt"], w=["srt"], tag=f"{tg}rc")
        d(lambda e: e.tensor_scalar(out=S[:, 3, 4:5], in0=S[:, 3, 0:1],
                                    scalar1=MOE_SCALE, scalar2=None,
                                    op0=ALU.mult),
          r=["srt"], w=["srt"], tag=f"{tg}rc2")
        d(lambda e: e.tensor_tensor(out=sb["cw"][:, tt, :],
                                    in0=S[:, 2, 2:6],
                                    in1=S[:, 3, 4:5].broadcast_to([128, 4]),
                                    op=ALU.mult),
          r=["srt"], w=["cw"], tag=f"{tg}cw")
        self.tr_fenced(
            1,
            lambda kc: (S[:, frow[(TOPK - 1) % 2], :], sb["ident_f32"],
                        ["srt", "ident_f32"]),
            lambda kc: ((ps[1][0:16, 0:128] if kc % 2 == 0 else
                         ps[2][0:16, 0:128]),
                        f"ps{1 + kc % 2}"),
            lambda kc, psap, pskey, fence: self.act(
                lambda e, psap=psap, tt=tt: e.activation(
                    out=sb["flags"][:, tt * 128:(tt + 1) * 128],
                    in_=psap, func=AF.Identity),
                r=[pskey, fence], w=["flags"], tag=f"{tg}evf"),
            f"{tg}TF")

    def _ffn_schedule(self):
        sb, ps = self.sb, self.ps
        d = self.dve
        d(lambda e: e.memset(sb["ones16k"], 1.0), w=["ones16k"], tag="fone")
        d(lambda e: e.tensor_tensor_scan(
            out=sb["pos"], data0=sb["ones16k"], data1=sb["flags"],
            initial=0.0, op0=ALU.mult, op1=ALU.add),
          r=["ones16k", "flags"], w=["pos"], tag="fpos")
        C = sb["cnt"]
        d(lambda e: e.tensor_copy(out=C[:, 0:1], in_=sb["pos"][:, TOK - 1:TOK]),
          r=["pos"], w=["cnt"], tag="fcnt")
        d(lambda e: e.tensor_scalar(out=C[:, 1:2], in0=C[:, 0:1],
                                    scalar1=FB / 2 - 0.5, scalar2=1.0 / FB,
                                    op0=ALU.add, op1=ALU.mult),
          r=["cnt"], w=["cnt"], tag="fblk")
        cnti = sb["sqd"].bitcast(I32)[0:16, 300:301]
        d(lambda e: e.tensor_copy(out=cnti, in_=C[:, 1:2]),
          r=["cnt"], w=["cnt"], tag="ftrunc")
        d(lambda e: e.tensor_copy(out=C[:, 3:4], in_=cnti),
          r=["cnt"], w=["cnt"], tag="fback")
        d(lambda e: e.tensor_scalar(out=C[:, 4:5], in0=C[:, 3:4],
                                    scalar1=float(FB), scalar2=None,
                                    op0=ALU.mult),
          r=["cnt"], w=["cnt"], tag="fpad")
        self.pe(lambda e: e.matmul(out=ps[2][0:16, 0:1], lhsT=sb["ltri16"],
                                   rhs=C[:, 4:5], start=True, stop=True),
                r=["cnt", "ltri16"], w=["ps2"], tag="fbase")
        self.act(lambda e: e.activation(out=C[:, 5:6], in_=ps[2][0:16, 0:1],
                                        func=AF.Identity),
                 r=["ps2"], w=["cnt"], tag="fevb")
        self.pe(lambda e: e.matmul(out=ps[3][0:16, 0:1], lhsT=sb["ltri16i"],
                                   rhs=C[:, 3:4], start=True, stop=True),
                r=["cnt", "ltri16i"], w=["ps3"], tag="fcumb")
        self.act(lambda e: e.activation(out=C[:, 6:7], in_=ps[3][0:16, 0:1],
                                        func=AF.Identity),
                 r=["ps3"], w=["cnt"], tag="fevc")
        self.tr_fenced(
            2,
            lambda kc: (C[:, 6 - kc:7 - kc], sb["ident_f32"][0:16, 0:16],
                        ["cnt", "ident_f32"]),
            lambda kc: (ps[4 + kc % 2][0:1, 0:16], f"ps{4 + kc % 2}"),
            lambda kc, psap, pskey, fence: self.act(
                lambda e, psap=psap, kc=kc: e.activation(
                    out=(sb["cumbr"] if kc == 0 else sb["baser"]),
                    in_=psap, func=AF.Identity),
                r=[pskey, fence], w=["sqd"],
                tag=("fevcr" if kc == 0 else "fevbr")),
            "FC")
        eidt = sb["sqd"][0:1, 128:128 + NB]
        eidb = sb["sqd"][0:1, 832:832 + NB]
        acc = [sb["eidr"], eidb]
        d(lambda e: e.memset(sb["eidr"], 0.0), w=["sqd"], tag="fez")
        for ei in range(E):
            d(lambda e, ei=ei: e.tensor_tensor(
                out=eidt,
                in0=sb["cumbr"][0:1, ei:ei + 1].broadcast_to([1, NB]),
                in1=sb["iota_nb"], op=ALU.is_le),
              r=["sqd", "iota_nb"], w=["sqd"], tag=f"fel{ei}")
            src = acc[ei % 2]
            dst = acc[(ei + 1) % 2]
            d(lambda e, src=src, dst=dst: e.tensor_tensor(
                out=dst, in0=src, in1=eidt, op=ALU.add),
              r=["sqd"], w=["sqd"], tag=f"fea{ei}")
        self.pe(lambda e: e.matmul(out=ps[6][:, 0:NB], lhsT=sb["ones_row"],
                                   rhs=sb["eidr"], start=True, stop=True),
                r=["sqd", "ones_row"], w=["ps6"], tag="fbw")
        d(lambda e: e.tensor_scalar(out=sb["widx"], in0=ps[6][:, 0:NB],
                                    scalar1=128.0, scalar2=sb["iota_p"],
                                    op0=ALU.mult, op1=ALU.add),
          r=["ps6", "iota_p"], w=["sqd"], tag="fwidx")
        self.pe(lambda e: e.matmul(out=ps[7][:, 0:16], lhsT=sb["ones_row"],
                                   rhs=sb["baser"], start=True, stop=True),
                r=["sqd", "ones_row"], w=["ps7"], tag="fbb")
        self.act(lambda e: e.activation(out=sb["base_bc"], in_=ps[7][:, 0:16],
                                        func=AF.Identity),
                 r=["ps7"], w=["sqd"], tag="fevbb")
        self.tr_fenced(
            TOK // 128,
            lambda tt: (sb["pos"][:, tt * 128:(tt + 1) * 128],
                        sb["ident_f32"][0:16, 0:16], ["pos", "ident_f32"]),
            lambda tt: (ps[tt % 2][0:128, 0:16], f"ps{tt % 2}"),
            lambda tt, psap, pskey, fence: self.act(
                lambda e, tt=tt, psap=psap: e.activation(
                    out=sb["post"][:, tt, :], in_=psap, func=AF.Identity),
                r=[pskey, fence], w=["post"], tag=f"fpe{tt}"),
            "FP")

    def _ffn_scatter(self):
        sb = self.sb
        d = self.dve
        S = sb["srt"]
        for tt in range(TOK // 128):
            b = tt % 2
            tg = f"sc{tt}_"
            xl = sb["x1lt"][:, b, :]
            self.dma("sp", xl, self.x1ln_hbm.ap()[tt * 128:(tt + 1) * 128, :],
                     r=["x1ln_hbm"], w=[f"x1lt{b}"], tag=f"{tg}ld")
            for j in range(TOPK):
                d(lambda e, tt=tt, j=j: e.tensor_tensor(
                    out=S[:, 5, :], in0=sb["iota16"],
                    in1=sb["eki"][:, tt, j:j + 1].broadcast_to([128, 16]),
                    op=ALU.is_equal),
                  r=["eki", "iota16"], w=["srt"], tag=f"{tg}sl{j}")
                d(lambda e, tt=tt: e.tensor_tensor(
                    out=S[:, 4, :], in0=S[:, 5, :], in1=sb["post"][:, tt, :],
                    op=ALU.mult),
                  r=["srt", "post"], w=["srt"], tag=f"{tg}pm{j}")
                d(lambda e: e.tensor_reduce(out=S[:, 2, 0:1], in_=S[:, 4, :],
                                            axis=AX, op=ALU.add),
                  r=["srt"], w=["srt"], tag=f"{tg}pr{j}")
                d(lambda e: e.tensor_tensor(out=S[:, 4, :], in0=S[:, 5, :],
                                            in1=sb["base_bc"], op=ALU.mult),
                  r=["srt", "sqd"], w=["srt"], tag=f"{tg}bm{j}")
                d(lambda e: e.tensor_reduce(out=S[:, 2, 1:2], in_=S[:, 4, :],
                                            axis=AX, op=ALU.add),
                  r=["srt"], w=["srt"], tag=f"{tg}br{j}")
                d(lambda e: e.tensor_tensor(out=S[:, 2, 2:3],
                                            in0=S[:, 2, 0:1],
                                            in1=S[:, 2, 1:2], op=ALU.add),
                  r=["srt"], w=["srt"], tag=f"{tg}ds{j}")
                d(lambda e, tt=tt, j=j: e.tensor_scalar(
                    out=sb["dsti"][:, tt, j:j + 1], in0=S[:, 2, 2:3],
                    scalar1=-1.0, scalar2=None, op0=ALU.add),
                  r=["srt"], w=["dsti"], tag=f"{tg}di{j}")
                self.pool(lambda e, tt=tt, j=j, xl=xl: e.indirect_dma_start(
                    out=self.xg_hbm.ap(),
                    out_offset=IndirectOffsetOnAxis(
                        ap=sb["dsti"][:, tt, j:j + 1], axis=0),
                    in_=xl, in_offset=None),
                    r=["dsti", f"x1lt{b}"], w=["xg_hbm"], dma=True,
                    tag=f"{tg}sx{j}")

    def _ffn_fetch(self, bi, buf):
        sb = self.sb
        for nm, tab in [("wg_t", self.gate_tab), ("wu_t", self.up_tab),
                        ("wd_t", self.down_tab)]:
            self.pool(lambda e, nm=nm, tab=tab, bi=bi, buf=buf:
                      e.indirect_dma_start(
                out=sb[f"{nm}{buf}"], out_offset=None,
                in_=tab.ap(),
                in_offset=IndirectOffsetOnAxis(
                    ap=sb["widx"][:, bi:bi + 1], axis=0)),
                r=["sqd", tab.name], w=[f"{nm}{buf}"], dma=True,
                tag=f"fw{bi}_{nm}")

    def _ffn_blocks(self):
        sb, ps = self.sb, self.ps
        NT4 = FB // 128
        self._ffn_fetch(0, 0)
        self._ffn_loadx(0, 0)
        for b in range(NB):
            buf = b % 2
            tg = f"b{b}_"
            if b + 1 < NB:
                self._ffn_fetch(b + 1, (b + 1) % 2)
                self._ffn_loadx(b + 1, (b + 1) % 2)
            for mc in range(NC_D):
                pg = mc % 2
                pu = 2 + mc % 2
                for kc in range(NC_D):
                    self.pe(lambda e, mc=mc, kc=kc, buf=buf, pg=pg: e.matmul(
                        out=ps[pg][:, 0:FB],
                        lhsT=sb[f"wg_t{buf}"][:,
                                        kc * 1024 + mc * 128:
                                        kc * 1024 + mc * 128 + 128],
                        rhs=sb[f"xg_fm{buf}"][:, kc, :], start=(kc == 0),
                        stop=(kc == NC_D - 1)),
                        r=[f"wg_t{buf}", f"xg_fm{buf}"], w=[f"ps{pg}"],
                        tag=f"{tg}g{mc}_{kc}")
                self.act(lambda e, mc=mc, pg=pg: e.activation(
                    out=sb["eg"][:, 0, :], in_=ps[pg][:, 0:FB],
                    func=AF.Exp, scale=-1.0),
                    r=[f"ps{pg}"], w=["eg"], tag=f"{tg}sa{mc}")
                self.act(lambda e: e.activation(
                    out=sb["tmu"][:, 0, :], in_=sb["eg"][:, 0, :],
                    func=AF.Ln, bias=sb["oner"]),
                    r=["eg", "oner"], w=["tmu"], tag=f"{tg}sb{mc}")
                self.act(lambda e, mc=mc: e.activation(
                    out=sb["sgate"][:, mc, :], in_=sb["tmu"][:, 0, :],
                    func=AF.Exp, scale=-1.0),
                    r=["tmu"], w=["sgate"], tag=f"{tg}sc{mc}")
                for kc in range(NC_D):
                    self.pe(lambda e, mc=mc, kc=kc, buf=buf, pu=pu: e.matmul(
                        out=ps[pu][:, 0:FB],
                        lhsT=sb[f"wu_t{buf}"][:,
                                        kc * 1024 + mc * 128:
                                        kc * 1024 + mc * 128 + 128],
                        rhs=sb[f"xg_fm{buf}"][:, kc, :], start=(kc == 0),
                        stop=(kc == NC_D - 1)),
                        r=[f"wu_t{buf}", f"xg_fm{buf}"], w=[f"ps{pu}"],
                        tag=f"{tg}u{mc}_{kc}")
                self.dve(lambda e, mc=mc, pu=pu: e.tensor_tensor(
                    out=sb["tmu"][:, 0, :], in0=sb["sgate"][:, mc, :],
                    in1=ps[pu][:, 0:FB], op=ALU.mult),
                    r=["sgate", f"ps{pu}"], w=["tmu"], tag=f"{tg}h{mc}")
                self.dve(lambda e, mc=mc, pg=pg: e.tensor_tensor(
                    out=sb["hub"][:, mc, :], in0=sb["tmu"][:, 0, :],
                    in1=ps[pg][:, 0:FB], op=ALU.mult),
                    r=["tmu", f"ps{pg}"], w=["hub"], tag=f"{tg}h2{mc}")
            for t4 in range(NT4):
                for n in range(2):
                    pd = 4 + (t4 * 2 + n) % 4
                    for kc in range(NC_D):
                        self.pe(lambda e, t4=t4, n=n, kc=kc, buf=buf, pd=pd:
                                e.matmul(
                            out=ps[pd][:, 0:512],
                            lhsT=sb["hub"][:, kc, t4 * 128:(t4 + 1) * 128],
                            rhs=sb[f"wd_t{buf}"][:,
                                           kc * 1024 + n * 512:
                                           kc * 1024 + n * 512 + 512],
                            start=(kc == 0), stop=(kc == NC_D - 1)),
                            r=["hub", f"wd_t{buf}"], w=[f"ps{pd}"],
                            tag=f"{tg}d{t4}_{n}_{kc}")
                    self.act(lambda e, t4=t4, n=n, buf=buf, pd=pd:
                             e.activation(
                        out=sb[f"ygb{buf}"][:, t4, n * 512:(n + 1) * 512],
                        in_=ps[pd][:, 0:512], func=AF.Identity),
                        r=[f"ps{pd}"], w=[f"ygb{buf}"], tag=f"{tg}ev{t4}_{n}")
                self.dma("sp", self.yg_hbm.ap()[b * FB + t4 * 128:
                                                b * FB + (t4 + 1) * 128, :],
                         sb[f"ygb{buf}"][:, t4, :],
                         r=[f"ygb{buf}"], w=["yg_hbm"], tag=f"{tg}st{t4}")

    def _ffn_loadx(self, bi, buf):
        sb = self.sb
        for t4 in range(FB // 128):
            b2 = t4 % 2
            xl = sb["x1lt"][:, b2, :]
            self.dma("sp", xl,
                     self.xg_hbm.ap()[bi * FB + t4 * 128:
                                      bi * FB + (t4 + 1) * 128, :],
                     r=["xg_hbm"], w=[f"x1lt{b2}"], tag=f"lx{bi}_{t4}")
            for kc in range(NC_D):
                self.dma("sp", sb[f"xg_fm{buf}"][:, kc,
                                           t4 * 128:(t4 + 1) * 128],
                         xl[:, kc * 128:(kc + 1) * 128],
                         r=[f"x1lt{b2}"], w=[f"xg_fm{buf}"], transpose=True,
                         tag=f"lt{bi}_{t4}_{kc}")

    def _ffn_shared_combine(self):
        sb, ps = self.sb, self.ps
        self.dma("sp", sb["shgw"],
                 self.shg_bf.ap().rearrange("(kc p) m -> p kc m", p=128),
                 r=["shg_bf"], w=["shgw"], tag="lshg")
        self.dma("sp", sb["shuw"],
                 self.shu_bf.ap().rearrange("(kc p) m -> p kc m", p=128),
                 r=["shu_bf"], w=["shuw"], tag="lshu")
        self.dma("sp", sb["shdw"],
                 self.shd_bf.ap().rearrange("(kc p) m -> p kc m", p=128),
                 r=["shd_bf"], w=["shdw"], tag="lshd")
        NCH = TOK // FB
        for ch in range(NCH):
            tg = f"h{ch}_"
            for t4 in range(NTT):
                b2 = t4 % 2
                tt = ch * NTT + t4
                xl = sb["x1lt"][:, b2, :]
                self.dma("sp", xl,
                         self.x1ln_hbm.ap()[tt * 128:(tt + 1) * 128, :],
                         r=["x1ln_hbm"], w=[f"x1lt{b2}"], tag=f"{tg}lx{t4}")
                for kc in range(NC_D):
                    self.dma("sp", sb["sxfm"][:, kc, t4 * 128:(t4 + 1) * 128],
                             xl[:, kc * 128:(kc + 1) * 128],
                             r=[f"x1lt{b2}"], w=["sxfm"], transpose=True,
                             tag=f"{tg}tx{t4}_{kc}")
            for mc in range(NC_I):
                pg = mc % 2
                pu = 2 + mc % 2
                for kc in range(NC_D):
                    self.pe(lambda e, mc=mc, kc=kc, pg=pg: e.matmul(
                        out=ps[pg][:, 0:FB],
                        lhsT=sb["shgw"][:, kc, mc * 128:(mc + 1) * 128],
                        rhs=sb["sxfm"][:, kc, :], start=(kc == 0),
                        stop=(kc == NC_D - 1)),
                        r=["shgw", "sxfm"], w=[f"ps{pg}"],
                        tag=f"{tg}g{mc}_{kc}")
                self.act(lambda e, mc=mc, pg=pg: e.activation(
                    out=sb["tsil"][:, 0, :], in_=ps[pg][:, 0:FB],
                    func=AF.Exp, scale=-1.0),
                    r=[f"ps{pg}"], w=["tsil"], tag=f"{tg}sa{mc}")
                self.act(lambda e: e.activation(
                    out=sb["eg4"][:, 0, :], in_=sb["tsil"][:, 0, :],
                    func=AF.Ln, bias=sb["oner"]),
                    r=["tsil", "oner"], w=["eg4"], tag=f"{tg}sb{mc}")
                self.act(lambda e: e.activation(
                    out=sb["tsil"][:, 0, :], in_=sb["eg4"][:, 0, :],
                    func=AF.Exp, scale=-1.0),
                    r=["eg4"], w=["tsil"], tag=f"{tg}sc{mc}")
                for kc in range(NC_D):
                    self.pe(lambda e, mc=mc, kc=kc, pu=pu: e.matmul(
                        out=ps[pu][:, 0:FB],
                        lhsT=sb["shuw"][:, kc, mc * 128:(mc + 1) * 128],
                        rhs=sb["sxfm"][:, kc, :], start=(kc == 0),
                        stop=(kc == NC_D - 1)),
                        r=["shuw", "sxfm"], w=[f"ps{pu}"],
                        tag=f"{tg}u{mc}_{kc}")
                self.dve(lambda e, mc=mc, pu=pu: e.tensor_tensor(
                    out=sb["eg4"][:, 0, :], in0=sb["tsil"][:, 0, :],
                    in1=ps[pu][:, 0:FB], op=ALU.mult),
                    r=["tsil", f"ps{pu}"], w=["eg4"], tag=f"{tg}h{mc}")
                self.dve(lambda e, mc=mc, pg=pg: e.tensor_tensor(
                    out=sb["shhub"][:, mc, :], in0=sb["eg4"][:, 0, :],
                    in1=ps[pg][:, 0:FB], op=ALU.mult),
                    r=["eg4", f"ps{pg}"], w=["shhub"], tag=f"{tg}h2{mc}")
            for t4 in range(NTT):
                b2 = t4 % 2
                tt = ch * NTT + t4
                x1 = sb["x1t"][:, b2, :]
                self.dma("sp", x1,
                         self.x1_hbm.ap()[tt * 128:(tt + 1) * 128, :],
                         r=["x1_hbm"], w=[f"x1t{b2}"], tag=f"{tg}l1{t4}")
                for j in range(TOPK):
                    self.pool(lambda e, tt=tt, j=j, b2=b2:
                              e.indirect_dma_start(
                        out=sb[f"gat{b2}"][:, j, :], out_offset=None,
                        in_=self.yg_hbm.ap(),
                        in_offset=IndirectOffsetOnAxis(
                            ap=sb["dsti"][:, tt, j:j + 1], axis=0)),
                        r=["dsti", "yg_hbm"], w=[f"gat{b2}"], dma=True,
                        tag=f"{tg}gy{t4}_{j}")
                ot = sb["ot"]
                ot2 = sb["ot2"]
                for n in range(2):
                    pd = 4 + (t4 * 2 + n) % 4
                    for kc in range(NC_I):
                        self.pe(lambda e, t4=t4, n=n, kc=kc, pd=pd: e.matmul(
                            out=ps[pd][:, 0:512],
                            lhsT=sb["shhub"][:, kc, t4 * 128:(t4 + 1) * 128],
                            rhs=sb["shdw"][:, kc, n * 512:(n + 1) * 512],
                            start=(kc == 0), stop=(kc == NC_I - 1)),
                            r=["shhub", "shdw"], w=[f"ps{pd}"],
                            tag=f"{tg}d{t4}_{n}_{kc}")
                    self.dve(lambda e, n=n, pd=pd, x1=x1: e.tensor_tensor(
                        out=ot[:, n * 512:(n + 1) * 512],
                        in0=ps[pd][:, 0:512],
                        in1=x1[:, n * 512:(n + 1) * 512], op=ALU.add),
                        r=[f"ps{pd}", f"x1t{b2}"], w=["sqd"],
                        tag=f"{tg}c{t4}_{n}")
                    pp = [ot[:, n * 512:(n + 1) * 512], ot2]
                    ppk = ["sqd", "ot2"]
                    for j in range(TOPK):
                        src = pp[j % 2]
                        dst = pp[(j + 1) % 2]
                        self.dve(lambda e, n=n, j=j, tt=tt, b2=b2, src=src,
                                 dst=dst: e.scalar_tensor_tensor(
                            out=dst,
                            in0=sb[f"gat{b2}"][:, j, n * 512:(n + 1) * 512],
                            scalar=sb["cw"][:, tt, j:j + 1],
                            in1=src,
                            op0=ALU.mult, op1=ALU.add),
                            r=[f"gat{b2}", "cw", ppk[j % 2]],
                            w=[ppk[(j + 1) % 2]],
                            tag=f"{tg}a{t4}_{n}_{j}")
                o = self.dma("sp", self.out.ap()[tt * 128:(tt + 1) * 128, :],
                             ot, r=["sqd"], w=["out"], tag=f"{tg}so{t4}")
                self.final.append(o)


    def build(self):
        self.build_init()
        if self.dev_route:
            self._ffn_routing()
            self.dump_raw("eki", self.sb["eki"], "xt0", F32)
            self.dump_raw("cw", self.sb["cw"], "xt0", F32)
            self.dump_raw("scall", self.sb["scores"], "xt0", F32)
            self.dump_raw("routr", self.sb["rout_rhs"], "rout_rhs", BF16)
            self.dump_raw("biase", self.sb["bias_e"], "bias_e", F32)
            self.dump_raw("io9", self.sb["io999"], "io999", F32)
            self.dump_raw("io16", self.sb["iota16"], "iota16", F32)
            self.dump_raw("srtf", self.sb["srt"], "xt0", F32)
            self.dump_raw("rtend", self.sb["rtmp"], "Rctx", F32)
            self.dma("sp", self.sb["rtmp"], self.w["router_w"].ap(),
                     r=["dbg_rtend"], w=["rtmp"], tag="reread_rw")
            self.dump_raw("rwend", self.sb["rtmp"], "Rctx", F32)
            nsig, nops = self.s.emit(self.nc, self.final)
            self.nsig, self.nops = nsig, nops
            return self.nc
        nseq = 1 if self.dev_chunks else SEQ_PER_CORE
        for seq in range(nseq):
            self.build_phase_a(seq)
        if self.dev_full:
            self.build_ffn()
            for o in self.s.ops:
                if o.tag.startswith("s1c7_sx"):
                    self.final.append(o)
            nsig, nops = self.s.emit(self.nc, self.final)
            self.nsig, self.nops = nsig, nops
            return self.nc
        if self.dev_chunks:
            sb = self.sb
            self.dump("xln_tm", sb["xln_tm"], "Rhc")
            self.dump("xln_fm", sb["xln_fm"], "Rxfm")
            self.dump("ctx", sb["ctx"], "Rctx")
            self.dump("zg", sb["zg"], "Rzg")
            self.dump("xcp", sb["xcp"], "Rxcp")
            self.dump("xdbl", sb["xdbl"], "xdbl")
            self.dump("lsg", sb["varg"], "Rvarg")
            self.dump("lsgx", sb["lsgx"], "Rlsgx")
            self.dump("nbc", sb["nbc"], "Rnbc")
            self.dump("cc", sb["cc"], "Rctx")
            self.dump("ybf", sb["ybf"], "Rsg")
            self.dump("state", sb["state"], "state")
            self.dump("hc", sb["hc"], "Rhc")
            self.dump("dAsn", self.sb["wg_t0"], "Rctx")
            self.dump("usn", self.sb["wg_t1"], "Rzg")
            self.dump("aneg", self.sb["aneg"], "aneg")

        if self.debug_phase_a:
            zs = self.sb["ybf"]
            outap = self.out.ap().rearrange("(r p) d -> p r d", p=128)
            self.dve(lambda e: e.memset(zs, 0.0), w=["ybf"], tag="zo")
            for i in range(0, TOK // 128, 8):
                o = self.dma("pool", outap[:, i:i + 8, :],
                             self.sb["ybf"]
                             .rearrange("p c q -> p (c q)")[:, 0:8 * D_MODEL]
                             .rearrange("p (n d) -> p n d", n=8),
                             r=["ybf"], w=["out"], tag=f"zout{i}")
            self.final.append(o)
            for o in self.s.ops:
                if o.tag.startswith("s1c7_sx"):
                    self.final.append(o)
        else:
            self.build_ffn()
        nsig, nops = self.s.emit(self.nc, self.final)
        self.nsig, self.nops = nsig, nops
        return self.nc



def kernel(**inputs):
    consts = host_constants()
    mk = MK(debug_phase_a=False)
    nc = mk.build()
    in_maps = []
    x = np.ascontiguousarray(inputs["L0_attn_x"].reshape(BATCH * SEQ,
                                                         D_MODEL))
    wmap = {
        "W_in": "L0_attn_W_in", "conv_w": "L0_attn_conv_w",
        "conv_b": "L0_attn_conv_b", "W_x": "L0_attn_W_x",
        "W_dt": "L0_attn_W_dt", "b_dt": "L0_attn_b_dt",
        "A_log": "L0_attn_A_log", "D_param": "L0_attn_D_param",
        "W_out": "L0_attn_W_out", "router_w": "L0_ffn_router_w",
        "bias": "L0_ffn_bias", "gate_w": "L0_ffn_gate_w",
        "up_w": "L0_ffn_up_w", "down_w": "L0_ffn_down_w",
        "sh_gate_w": "L0_ffn_sh_gate_w", "sh_up_w": "L0_ffn_sh_up_w",
        "sh_down_w": "L0_ffn_sh_down_w",
    }
    for core in range(N_CORES):
        m = {"x_in": x[core * TOK:(core + 1) * TOK]}
        for k, v in wmap.items():
            m[k] = np.ascontiguousarray(inputs[v])
        m.update(consts)
        in_maps.append(m)
    res = run_bass_kernel_spmd(nc, in_maps, core_ids=list(range(N_CORES)))
    out = np.concatenate([np.asarray(r["out"]) for r in res.results], axis=0)
    return out.reshape(BATCH, SEQ, D_MODEL).astype(np.float32)


# version 03
# speedup vs baseline: 1.2477x; 1.2477x over previous
import numpy as np
import ml_dtypes
import concourse.bass as bass
import concourse.mybir as mybir
from concourse.bass_utils import run_bass_kernel_spmd
from concourse.bass_core_rust import IndirectOffsetOnAxis

F32 = mybir.dt.float32
BF16 = mybir.dt.bfloat16
I32 = mybir.dt.int32
AF = mybir.ActivationFunctionType
ALU = mybir.AluOpType
AX = mybir.AxisListType.X

D_MODEL = 1024
D_INNER = 2048
D_STATE = 16
D_CONV = 4
SEQ = 4096
BATCH = 16
N_CORES = 8
SEQ_PER_CORE = BATCH // N_CORES
TOK = SEQ_PER_CORE * SEQ
NC_D = D_MODEL // 128
NC_I = D_INNER // 128
Q = 512
NTT = Q // 128
NCHUNK = SEQ // Q
E = 16
NG, EPG, TKG, TOPK = 4, 4, 2, 4
MOE_SCALE = 2.5
FB = 512
NB = (TOPK * TOK) // FB + (E - 1)
NSLOT = NB * FB
LN_EPS = 1e-5

ENGINES = ("pe", "act", "dve", "pool", "sp")
ENG_ATTR = {"pe": "tensor", "act": "scalar", "dve": "vector",
            "pool": "gpsimd", "sp": "sync"}


class Op:
    __slots__ = ("idx", "eng", "fn", "reads", "writes", "is_dma", "deps",
                 "sig_id", "tag")

    def __init__(self, idx, eng, fn, reads, writes, is_dma, tag):
        self.idx = idx
        self.eng = eng
        self.fn = fn
        self.reads = reads
        self.writes = writes
        self.is_dma = is_dma
        self.deps = set()
        self.sig_id = None
        self.tag = tag


class Sched:
    def __init__(self):
        self.ops = []
        self.last_writer = {}
        self.readers = {}
        self.keymap = {}

    def op(self, eng, fn, r=(), w=(), dma=False, tag=""):
        km = self.keymap

        def expand(keys):
            out = []
            for k in keys:
                v = km.get(k, k)
                if isinstance(v, (tuple, list)):
                    out.extend(v)
                else:
                    out.append(v)
            return tuple(out)
        o = Op(len(self.ops), eng, fn, expand(r), expand(w), dma, tag)
        for k in o.reads:
            lw = self.last_writer.get(k)
            if lw is not None:
                o.deps.add(lw.idx)
            self.readers.setdefault(k, []).append(o)
        for k in o.writes:
            lw = self.last_writer.get(k)
            if lw is not None:
                o.deps.add(lw.idx)
            for rd in self.readers.get(k, ()):
                o.deps.add(rd.idx)
            self.readers[k] = []
            self.last_writer[k] = o
        o.deps.discard(o.idx)
        self.ops.append(o)
        return o

    def emit(self, nc, final_outputs):
        ops = self.ops
        needs_sig = [o.is_dma for o in ops]
        for o in ops:
            for d in o.deps:
                dep = ops[d]
                if (dep.eng == o.eng == "pe" and not dep.is_dma
                        and not o.is_dma):
                    continue
                needs_sig[d] = True
        for o in final_outputs:
            needs_sig[o.idx] = True

        level = [0] * len(ops)
        for o in ops:
            lv = 0
            for d in o.deps:
                if level[d] >= lv:
                    lv = level[d] + 1
            level[o.idx] = lv
        order = sorted(range(len(ops)), key=lambda i: (level[i], i))

        DMA_LANES = 8
        counters = {}
        qord = {}
        for i in order:
            o = ops[i]
            if needs_sig[o.idx]:
                if o.is_dma:
                    n = qord.get(o.eng, 0)
                    qord[o.eng] = n + 1
                    key = ("d", o.eng, n % DMA_LANES)
                else:
                    key = ("c", o.eng)
                counters[key] = counters.get(key, 0) + 1
                o.sig_id = (key, counters[key])

        sems, ctxs = {}, []
        for key in counters:
            cm = nc.semaphore("sem_" + "_".join(str(k) for k in key))
            sems[key] = cm.__enter__()
            ctxs.append(cm)

        block_cm = nc.Block()
        block = block_cm.__enter__()
        per_eng = {e: [ops[i] for i in order if ops[i].eng == e]
                   for e in ENGINES}
        for eng in ENGINES:
            eng_ops = per_eng[eng]
            if not eng_ops:
                continue

            def body(engine, eng=eng, eng_ops=eng_ops):
                water = {}
                for o in eng_ops:
                    want = {}
                    if o.is_dma and o.sig_id is not None:
                        key, val = o.sig_id
                        if val > 1:
                            want[key] = val - 1
                    for d in sorted(o.deps):
                        dep = ops[d]
                        if dep.sig_id is None:
                            continue
                        key, val = dep.sig_id
                        want[key] = max(want.get(key, 0), val)
                    for key, val in want.items():
                        v = val * 16 if key[0] == "d" else val
                        if water.get(key, 0) >= v:
                            continue
                        engine.wait_ge(sems[key], v)
                        water[key] = v
                    inst = o.fn(engine)
                    if o.sig_id is not None:
                        key, val = o.sig_id
                        if inst is None:
                            raise RuntimeError(f"op {o.tag}: no instruction")
                        inst.then_inc(sems[key], 16 if key[0] == "d" else 1)

            getattr(block, ENG_ATTR[eng])(body)

        def final_body(engine):
            want = {}
            for o in final_outputs:
                key, val = o.sig_id
                v = val * 16 if key[0] == "d" else val
                want[key] = max(want.get(key, 0), v)
            for key, v in want.items():
                engine.wait_ge(sems[key], v)
        block.sync(final_body)

        block_cm.__exit__(None, None, None)
        for cm in reversed(ctxs):
            cm.__exit__(None, None, None)
        return sum(needs_sig), len(ops)


def host_constants():
    c = {}
    c["ident_f32"] = np.eye(128, dtype=np.float32)
    c["ident_bf"] = np.eye(128, dtype=np.float32).astype(ml_dtypes.bfloat16)
    c["ones_row"] = np.ones((1, 128), dtype=np.float32)
    c["ltri16"] = np.triu(np.ones((16, 16), np.float32), 1)
    c["ltri16i"] = np.triu(np.ones((16, 16), np.float32), 0)
    c["io999"] = 999.0 - np.tile(np.arange(16, dtype=np.float32)[None, :],
                                 (128, 1))
    c["iota_p"] = np.arange(128, dtype=np.float32)[:, None]
    c["iota16"] = np.tile(np.arange(16, dtype=np.float32)[None, :], (128, 1))
    c["iota_nb"] = np.arange(NB, dtype=np.float32)[None, :]
    sel = np.zeros((33, 33 * 128), np.float32)
    for r in range(33):
        sel[r, r * 128:(r + 1) * 128] = 1.0
    c["sel_bf"] = sel.astype(ml_dtypes.bfloat16)
    return c


class MK:
    def __init__(self, debug_phase_a=False, dev_chunks=None, dev_full=False,
                 dev_route=False):
        self.debug_phase_a = debug_phase_a or dev_full
        self.dev_full = dev_full
        self.dev_route = dev_route
        self.dev_chunks = dev_chunks
        self.dumps = {}
        self.nc = bass.Bass("TRN2", target_bir_lowering=False)
        self.s = Sched()
        self.final = []
        self._declare_io()
        self._alloc_sbuf()
        self._alloc_psum()
        self.s.keymap = dict(self.key)

    def _declare_io(self):
        nc = self.nc
        D = lambda n, sh, dt, kind: nc.dram_tensor(n, sh, dt, kind=kind)
        self.x_in = D("x_in", [TOK, D_MODEL], F32, "ExternalInput")
        self.w = {}
        for name, sh in [
            ("W_in", [D_MODEL, 2 * D_INNER]), ("conv_w", [D_INNER, 1, D_CONV]),
            ("conv_b", [D_INNER]), ("W_x", [D_INNER, 2 * D_STATE + 1]),
            ("W_dt", [1, D_INNER]), ("b_dt", [D_INNER]),
            ("A_log", [D_INNER, D_STATE]), ("D_param", [D_INNER]),
            ("W_out", [D_INNER, D_MODEL]),
            ("router_w", [E, D_MODEL]), ("bias", [E]),
            ("gate_w", [E, D_MODEL, 1024]), ("up_w", [E, D_MODEL, 1024]),
            ("down_w", [E, 1024, D_MODEL]),
            ("sh_gate_w", [D_MODEL, 2048]), ("sh_up_w", [D_MODEL, 2048]),
            ("sh_down_w", [2048, D_MODEL]),
            ("ident_f32", [128, 128]), ("ones_row", [1, 128]),
            ("ltri16", [16, 16]), ("ltri16i", [16, 16]),
            ("iota_p", [128, 1]), ("iota16", [128, 16]), ("iota_nb", [1, NB]),
            ("io999", [128, 16]),
        ]:
            self.w[name] = D(name, sh, F32, "ExternalInput")
        self.w["ident_bf"] = D("ident_bf", [128, 128], BF16, "ExternalInput")
        self.w["sel_bf"] = D("sel_bf", [33, 33 * 128], BF16, "ExternalInput")
        self.out = D("out", [TOK, D_MODEL], F32, "ExternalOutput")

        sk = "ExternalOutput" if self.debug_phase_a else "Internal"
        if getattr(self, "dev_route", False):
            self.x1_hbm = D("x1_hbm", [TOK, D_MODEL], F32, "ExternalInput")
            self.x1ln_hbm = D("x1ln_hbm", [TOK, D_MODEL], BF16,
                              "ExternalInput")
        else:
            self.x1_hbm = D("x1_hbm", [TOK, D_MODEL], F32, sk)
            self.x1ln_hbm = D("x1ln_hbm", [TOK, D_MODEL], BF16, sk)
        self.xg_hbm = D("xg_hbm", [NSLOT, D_MODEL], BF16, sk)
        self.yg_hbm = D("yg_hbm", [NSLOT, D_MODEL], BF16, sk)
        self.win_bf = D("win_bf", [D_MODEL, 2 * D_INNER], BF16, "Internal")
        self.wout_bf = D("wout_bf", [D_INNER, D_MODEL], BF16, "Internal")
        self.gate_tab = D("gate_tab", [E * 128, 8 * 1024], BF16, "Internal")
        self.up_tab = D("up_tab", [E * 128, 8 * 1024], BF16, "Internal")
        self.down_tab = D("down_tab", [E * 128, 8 * 1024], BF16, "Internal")
        self.shg_bf = D("shg_bf", [D_MODEL, 2048], BF16, "Internal")
        self.shu_bf = D("shu_bf", [D_MODEL, 2048], BF16, "Internal")
        self.shd_bf = D("shd_bf", [2048, D_MODEL], BF16, "Internal")

    def _alloc_sbuf(self):
        nc = self.nc
        self.sb = {}
        self.key = {}

        def T(name, shape, dt, key=None):
            self.sb[name] = nc.alloc_sbuf_tensor("sb_" + name, shape, dt).ap()
            self.key[name] = key or name

        T("ident_f32", [128, 128], F32)
        T("ident_bf", [128, 128], BF16)
        T("ones_row", [1, 128], F32)
        T("ltri16", [16, 16], F32)
        T("ltri16i", [16, 16], F32)
        T("iota_p", [128, 1], F32)
        T("iota16", [128, 16], F32)
        T("iota_nb", [1, NB], F32)
        T("io999", [128, 16], F32)
        T("sel_bf", [33, 33 * 128], BF16)
        T("convw", [128, NC_I, D_CONV], F32)
        T("convb", [128, NC_I], F32)
        T("dparam", [128, NC_I], F32)
        T("wdt", [128, NC_I], F32)
        T("bdt", [128, NC_I], F32)
        T("aneg", [128, NC_I, D_STATE], F32)
        T("wx_lhsT", [128, NC_I, 33], BF16)
        T("rout_rhs", [128, NC_D, E], BF16)
        T("bias_e", [128, E], F32)
        T("brow", [1, E], F32)
        T("lnrow", [128, 16], F32)
        T("epsr", [128, 1], F32)
        T("oner", [128, 1], F32)
        T("state", [128, NC_I, D_STATE], F32)
        T("xdbl", [33, Q], BF16)
        T("xt", [128, 2, D_MODEL], F32)
        T("sqd", [128, D_MODEL], F32)
        T("win_m", [128, 2, NC_D, 128], BF16)
        T("wout_k", [128, 2, D_MODEL], BF16)
        T("ctx2", [128, NC_I, 3], BF16)
        T("dA", [128, 2, Q], F32)
        T("ub", [128, 2, Q], BF16)
        T("prod", [128, 2, Q], BF16)
        T("x1t", [128, 2, D_MODEL], F32)
        T("x1lt", [128, 2, D_MODEL], BF16)

        AR = 8240 + 8 * 8192 + 4096
        arena = nc.alloc_sbuf_tensor("sb_arena", [128, AR], BF16).ap()
        self.arena = arena
        off = [0]

        def V(name, shape, key, elems=None, at=None):
            n = elems if elems is not None else int(np.prod(shape[1:]))
            o = at if at is not None else off[0]
            ap = arena[:, o:o + n]
            if len(shape) == 3:
                ap = ap.rearrange("p (a b) -> p a b", a=shape[1])
            self.sb[name] = ap
            self.key[name] = key
            if at is None:
                off[0] = o + n
            return o

        o_ctx = V("ctx", [128, NC_I, Q + 3], "Rctx")
        V("cc", [128, D_STATE, Q], "Rctx", at=o_ctx)
        o_zg = V("zg", [128, NC_I, Q], "Rzg", at=8240)
        off[0] = 8240 + 8192
        o_ctv = V("ctv", [128, NC_I, Q], "Rctv")
        V("ybf", [128, NC_I, Q], "Rctv", at=o_ctv)
        V("xcp", [128, NC_I, Q], "Rxcp")
        V("varg", [128, NC_I, Q], "Rvarg")
        o_sg = V("sg", [128, NC_I, Q], "Rsg")
        V("szg", [128, NC_I, Q], "Rsg", at=o_sg)
        V("lsgx", [128, NC_I, Q], "Rlsgx")
        V("nbc", [128, D_STATE, Q], "Rnbc")
        o_hc = V("hc", [128, D_STATE, Q], "Rhc")
        self.sb["rtmp"] = arena[:, 0:2048].bitcast(F32)[64:80, :]
        self.key["rtmp"] = "Rctx"

        V("xln_tm", [128, NTT, D_MODEL], "Rhc", at=o_hc)
        V("xln_fm", [128, NC_D, Q], "Rxfm", elems=NC_D * Q)
        self.arena_used = off[0]

        for nm, pair in [("ctx", ("Rctx", "Rzg")), ("cc", ("Rctx",)),
                         ("lsgx", ("RlsgxA", "RlsgxB")),
                         ("nbc", ("RnbcA", "RnbcB")),
                         ("hc", ("RhcA", "RhcB")),
                         ("xln_tm", ("RhcA",))]:
            self.key[nm] = pair
        self.key["ctx"] = ("Rctx",)
        self.key["rtmp"] = ("Rctx",)

        def VF(name, shape, at, keys, dtype=BF16):
            n = int(np.prod(shape[1:]))
            if dtype == BF16:
                ap = arena[:, at:at + n]
            else:
                ap = arena[:, at:at + 2 * n].bitcast(F32)
            if shape[0] != 128:
                ap = ap[0:shape[0]]
            if len(shape) == 3:
                ap = ap.rearrange("p (a b) -> p a b", a=shape[1])
            elif len(shape) == 4:
                ap = ap.rearrange("p (a b c) -> p a b c", a=shape[1],
                                  b=shape[2])
            self.sb[name] = ap
            self.key[name] = keys

        O_ZG, O_CTV, O_XCP, O_VARG, O_SG, O_LSGX, O_NBC, O_HC, O_XFM = (
            8240, 16432, 24624, 32816, 41008, 49200, 57392, 65584, 73776)
        VF("flags", [16, TOK], 0, ("Rctx", "Rzg"), F32)
        VF("pos", [16, TOK], O_CTV, ("Rctv", "Rxcp"), F32)
        VF("ones16k", [16, TOK], O_VARG, ("Rvarg", "Rsg"), F32)
        VF("post", [128, 64, E], O_HC, ("RhcA",), F32)
        VF("rfm", [128, NC_D, 128], O_LSGX, ("RlsgxA",))
        VF("wg_t0", [128, 8 * 1024], 0, ("Rctx",))
        VF("wg_t1", [128, 8 * 1024], O_ZG, ("Rzg",))
        VF("wu_t0", [128, 8 * 1024], O_CTV, ("Rctv",))
        VF("wu_t1", [128, 8 * 1024], O_XCP, ("Rxcp",))
        VF("wd_t0", [128, 8 * 1024], O_VARG, ("Rvarg",))
        VF("wd_t1", [128, 8 * 1024], O_SG, ("Rsg",))
        VF("xg_fm0", [128, NC_D, FB], O_LSGX, ("RlsgxA",))
        VF("xg_fm1", [128, NC_D, FB], O_LSGX + 4096, ("RlsgxB",))
        VF("sgate", [128, NC_D, FB], O_NBC, ("RnbcA",))
        VF("hub", [128, NC_D, FB], O_NBC + 4096, ("RnbcB",))
        VF("ygb0", [128, NTT, D_MODEL], O_HC, ("RhcA",))
        VF("ygb1", [128, NTT, D_MODEL], O_HC + 4096, ("RhcB",))
        VF("shgw", [128, NC_D, 2048], 0, ("Rctx", "Rzg"))
        VF("shuw", [128, NC_D, 2048], O_CTV, ("Rctv", "Rxcp"))
        VF("shdw", [128, NC_I, 1024], O_VARG, ("Rvarg", "Rsg"))
        VF("sxfm", [128, NC_D, FB], O_LSGX, ("RlsgxA",))
        VF("tsil", [128, 1, FB], O_XFM, ("Rxfm",))
        VF("gat0", [128, TOPK, D_MODEL], O_NBC, ("RnbcA",))
        VF("gat1", [128, TOPK, D_MODEL], O_NBC + 4096, ("RnbcB",))
        VF("shhub", [128, NC_I, FB], O_HC, ("RhcA", "RhcB"))
        VF("eg", [128, 1, FB], O_XFM, ("Rxfm",), F32)
        VF("tmu", [128, 1, FB], O_XFM + 2048, ("Rxfm",), F32)
        VF("eg4", [128, 1, FB], O_LSGX + 4096, ("RlsgxB",), F32)
        xtf = self.sb["xt"].rearrange("p a b -> p (a b)")
        self.sb["cw"] = xtf[:, 0:256].rearrange("p (a b) -> p a b", a=64)
        self.sb["eki"] = xtf[:, 256:512].rearrange("p (a b) -> p a b", a=64)
        self.sb["dstf"] = xtf[:, 512:768].rearrange("p (a b) -> p a b", a=64)
        self.sb["dsti"] = xtf[:, 768:1024].bitcast(I32).rearrange(
            "p (a b) -> p a b", a=64)
        self.sb["scores"] = xtf[:, 1024:1040]
        self.sb["ot2"] = xtf[:, 1168:1680]
        self.key["ot2"] = ("xt0", "xt1")
        self.sb["srt"] = xtf[:, 1040:1168].rearrange(
            "p (a b) -> p a b", a=8)
        for nm in ["cw", "eki", "dstf", "dsti", "scores", "srt"]:
            self.key[nm] = ("xt0", "xt1")
        sq = self.sb["sqd"]
        self.sb["ot"] = sq[:, 0:1024]
        self.sb["alog"] = sq[:, 256:512].rearrange("p (c s) -> p c s", c=NC_I)
        self.sb["apos"] = sq[:, 512:768].rearrange("p (c s) -> p c s", c=NC_I)
        for nm in ["alog", "apos"]:
            self.key[nm] = "sqd"

        self.sb["eidr"] = sq[0:1, 0:NB]
        self.sb["cnt"] = sq[0:16, 256:264]
        self.sb["base_bc"] = sq[:, 320:336]
        self.sb["cumbr"] = sq[0:1, 384:400]
        self.sb["baser"] = sq[0:1, 448:464]
        self.sb["widx"] = sq.bitcast(I32)[:, 512:512 + NB]
        self.sb["ones16c"] = sq[0:16, 640:641]
        for nm in ["eidr", "cnt", "base_bc", "cumbr", "baser", "widx",
                   "ones16c"]:
            self.key[nm] = "sqd"
        self.sbuf_free_after_a = nc.sbuf_bytes_remaining

    def kn(self, *names):
        return [self.key.get(n, n) for n in names]

    def _alloc_psum(self):
        self.ps = {}
        for i in range(8):
            self.ps[i] = self.nc.alloc_psum_tensor(f"ps{i}", [128, 512],
                                                   F32).ap()

    def dma(self, q, out, in_, r=(), w=(), tag="", **kw):
        if not getattr(self, "_warm_done", None):
            self._warm_done = set()
        if q not in self._warm_done and hasattr(self, "_warm_gated"):
            self._warm_done.add(q)
            r = tuple(r) + ("warm",)
        return self.s.op(q, lambda e: e.dma_start(out=out, in_=in_, **kw),
                         r=r, w=w, dma=True, tag=tag)

    def act(self, fn, r=(), w=(), tag=""):
        return self.s.op("act", fn, r=r, w=w, tag=tag)

    def dve(self, fn, r=(), w=(), tag=""):
        return self.s.op("dve", fn, r=r, w=w, tag=tag)

    def pool(self, fn, r=(), w=(), tag="", dma=False):
        return self.s.op("pool", fn, r=r, w=w, tag=tag, dma=dma)

    def pe(self, fn, r=(), w=(), tag=""):
        return self.s.op("pe", fn, r=r, w=w, tag=tag)

    def tr_fenced(self, n, srcs, psel, outs, tag):
        for kc in range(n + 1):
            src = srcs(kc % n)
            self.pe(lambda e, kc=kc, src=src, psel=psel: e.matmul(
                out=psel(kc)[0], lhsT=src[0], rhs=src[1],
                start=True, stop=True),
                r=list(src[2]) + ([f"{tag}F{kc - 1}"] if kc else []),
                w=[psel(kc)[1], f"{tag}F{kc}"],
                tag=f"{tag}tr{kc}")
            if kc > 0:
                outs(kc - 1, psel(kc - 1)[0], psel(kc - 1)[1],
                     f"{tag}F{kc}")

    def dump_raw(self, name, ap, key, dtype):
        sh = list(ap.shape)
        t = self.nc.dram_tensor("dbg_" + name, sh, dtype,
                                kind="ExternalOutput")
        o = self.dma("sp", t.ap(), ap, r=[key], w=[f"dbg_{name}"],
                     tag=f"dump_{name}")
        self.final.append(o)

    def dump(self, name, ap, key):
        sh = list(ap.shape)
        t = self.nc.dram_tensor("dbg_" + name, sh, F32,
                                kind="ExternalOutput")
        o = self.dma("pool", t.ap(), ap, r=[key], w=[f"dbg_{name}"],
                     tag=f"dump_{name}")
        self.final.append(o)

    def build_init(self):
        sb, w, ps = self.sb, self.w, self.ps
        for i in range(3):
            self.act(lambda e: e.activation(
                out=sb["sqd"][:, 0:1024], in_=sb["sqd"][:, 0:1024],
                func=AF.Exp),
                r=["warm"], w=["warm", "sqd"], tag=f"warm{i}")
        self._warm_gated = False
        for name in ["ident_f32", "ident_bf", "ones_row", "ltri16", "ltri16i",
                     "iota_p", "iota16", "iota_nb", "sel_bf", "io999"]:
            self.dma("sp", sb[name], w[name].ap(), w=[name], tag=f"ld_{name}")
        self.dma("sp", sb["convw"],
                 w["conv_w"].ap().rearrange("(c p) one k -> p c (one k)",
                                            p=128),
                 w=["convw"], tag="ld_convw")
        for nm, src in [("convb", "conv_b"), ("dparam", "D_param"),
                        ("bdt", "b_dt")]:
            self.dma("sp", sb[nm],
                     w[src].ap().rearrange("(c p) -> p c", p=128),
                     w=[nm], tag=f"ld_{nm}", allow_slow_non_contiguous=True)
        self.dma("sp", sb["wdt"],
                 w["W_dt"].ap().rearrange("one (c p) -> p (one c)", p=128),
                 w=["wdt"], tag="ld_wdt", allow_slow_non_contiguous=True)
        self.dma("sp", sb["alog"],
                 w["A_log"].ap().rearrange("(c p) s -> p c s", p=128),
                 w=["alog"], tag="ld_alog")
        self.act(lambda e: e.activation(
            out=sb["apos"].rearrange("p c s -> p (c s)"),
            in_=sb["alog"].rearrange("p c s -> p (c s)"), func=AF.Exp),
            r=["alog"], w=["apos"], tag="exp_alog")
        self.dve(lambda e: e.tensor_scalar(
            out=sb["aneg"].rearrange("p c s -> p (c s)"),
            in0=sb["apos"].rearrange("p c s -> p (c s)"),
            scalar1=-1.0, scalar2=None, op0=ALU.mult),
            r=["apos"], w=["aneg"], tag="neg_alog")
        self.dma("pool", sb["wx_lhsT"],
                 w["W_x"].ap().rearrange("(c p) m -> p c m", p=128),
                 w=["wx_lhsT"], tag="ld_wx")
        self.dma("sp", sb["rtmp"], w["router_w"].ap(), w=["rtmp"],
                 tag="ld_router")
        self.tr_fenced(
            NC_D,
            lambda kc: (sb["rtmp"][:, kc * 128:(kc + 1) * 128],
                        sb["ident_f32"][64:80, 64:80], ["rtmp", "ident_f32"]),
            lambda kc: (ps[kc % 2][:, 0:16], f"ps{kc % 2}"),
            lambda kc, psap, pskey, fence: self.act(
                lambda e, kc=kc, psap=psap: e.activation(
                    out=sb["rout_rhs"][:, kc, :], in_=psap,
                    func=AF.Identity),
                r=[pskey, fence], w=["rout_rhs"], tag=f"ev_rout{kc}"),
            "Rt")
        self.dma("sp", sb["brow"], w["bias"].ap().unsqueeze(0), w=["brow"],
                 tag="ld_bias")
        self.pe(lambda e: e.matmul(out=ps[1][:, 0:E], lhsT=sb["ones_row"],
                                   rhs=sb["brow"], start=True, stop=True),
                r=["brow", "ones_row"], w=["ps1"], tag="bc_bias")
        self.act(lambda e: e.activation(out=sb["bias_e"], in_=ps[1][:, 0:E],
                                        func=AF.Identity),
                 r=["ps1"], w=["bias_e"], tag="ev_bias")
        self.dma("pool", self.win_bf.ap(), w["W_in"].ap(), w=["win_bf"],
                 tag="pre_win")
        self.dma("pool", self.wout_bf.ap(), w["W_out"].ap(), w=["wout_bf"],
                 tag="pre_wout")
        for tab, src in [("gate_tab", "gate_w"), ("up_tab", "up_w"),
                         ("down_tab", "down_w")]:
            tb = getattr(self, tab).ap().rearrange(
                "(e p) (kc m) -> e p kc m", p=128, kc=8)
            sr = w[src].ap().rearrange("e (kc p) m -> e p kc m", p=128)
            for ei in range(E):
                self.dma("pool", tb[ei], sr[ei], w=[tab],
                         tag=f"pre_{tab}{ei}")
        self.dma("pool", self.shg_bf.ap(), w["sh_gate_w"].ap(), w=["shg_bf"],
                 tag="pre_shg")
        self.dma("pool", self.shu_bf.ap(), w["sh_up_w"].ap(), w=["shu_bf"],
                 tag="pre_shu")
        self.dma("pool", self.shd_bf.ap(), w["sh_down_w"].ap(), w=["shd_bf"],
                 tag="pre_shd")
        self.dve(lambda e: e.memset(sb["ybf"], 0.0), w=["ybf"], tag="zt")
        xg = self.xg_hbm.ap().rearrange("(r p) d -> p r d", p=128)
        nrow = NSLOT // 128
        step = 8
        for i in range(0, nrow, step):
            n = min(step, nrow - i)
            self.dma("sp", xg[:, i:i + n, :],
                     sb["ybf"].rearrange("p c q -> p (c q)")[:, 0:n * D_MODEL]
                     .rearrange("p (n d) -> p n d", n=n),
                     r=["ybf"], w=["xg_hbm"], tag=f"zxg{i}")
        self.dve(lambda e: e.memset(sb["state"], 0.0), w=["state"], tag="zst")
        self.dve(lambda e: e.memset(sb["epsr"], LN_EPS), w=["epsr"],
                 tag="ldeps")
        self.dve(lambda e: e.memset(sb["oner"], 1.0), w=["oner"],
                 tag="ldone")

    def _ln_tile(self, src_ap, dst_ap, rkeys, wkey, tag):
        sb = self.sb
        ln = sb["lnrow"]
        self.dve(lambda e: e.reduce_sum(out=ln[:, 0:1], in_=src_ap, axis=AX),
                 r=rkeys, w=["lnrow"], tag=f"{tag}_s")
        self.act(lambda e: e.activation(out=sb["sqd"], in_=src_ap,
                                        func=AF.Square,
                                        accum_out=ln[:, 1:2]),
                 r=rkeys, w=["lnrow", "sqd"], tag=f"{tag}_q")
        self.dve(lambda e: e.tensor_scalar(
            out=ln[:, 2:3], in0=ln[:, 0:1], scalar1=-1.0 / D_MODEL,
            scalar2=None, op0=ALU.mult),
            r=["lnrow"], w=["lnrow"], tag=f"{tag}_m")
        self.dve(lambda e: e.tensor_tensor(
            out=ln[:, 3:4], in0=ln[:, 2:3], in1=ln[:, 2:3], op=ALU.mult),
            r=["lnrow"], w=["lnrow"], tag=f"{tag}_m2")
        self.dve(lambda e: e.tensor_scalar(
            out=ln[:, 4:5], in0=ln[:, 1:2], scalar1=1.0 / D_MODEL,
            scalar2=None, op0=ALU.mult),
            r=["lnrow"], w=["lnrow"], tag=f"{tag}_v1")
        self.dve(lambda e: e.tensor_tensor(
            out=ln[:, 5:6], in0=ln[:, 4:5], in1=ln[:, 3:4], op=ALU.subtract),
            r=["lnrow"], w=["lnrow"], tag=f"{tag}_v")
        self.act(lambda e: e.activation(out=ln[:, 6:7], in_=ln[:, 5:6],
                                        func=AF.Ln, bias=sb["epsr"]),
                 r=["lnrow", "epsr"], w=["lnrow"], tag=f"{tag}_l")
        self.act(lambda e: e.activation(out=ln[:, 7:8], in_=ln[:, 6:7],
                                        func=AF.Exp, scale=-0.5),
                 r=["lnrow"], w=["lnrow"], tag=f"{tag}_r")
        self.dve(lambda e: e.tensor_tensor(
            out=sb["sqd"][:, 0:D_MODEL], in0=src_ap,
            in1=ln[:, 2:3].broadcast_to([128, D_MODEL]), op=ALU.add),
            r=list(rkeys) + ["lnrow"], w=["sqd"], tag=f"{tag}_a1")
        self.dve(lambda e: e.tensor_tensor(
            out=dst_ap, in0=sb["sqd"][:, 0:D_MODEL],
            in1=ln[:, 7:8].broadcast_to([128, D_MODEL]), op=ALU.mult),
            r=["sqd", "lnrow"], w=[wkey], tag=f"{tag}_a2")

    def build_phase_a(self, seq):
        sb = self.sb
        if seq > 0:
            self.dve(lambda e: e.memset(sb["state"], 0.0), w=["state"],
                     tag=f"zs{seq}")
        nch = self.dev_chunks or NCHUNK
        for k in range(nch):
            t0 = seq * SEQ + k * Q
            tag = f"s{seq}c{k}_"
            self._chunk_ln_win(t0, k, tag)
            self._chunk_conv_dt(k, tag)
            self._chunk_scan(k, tag)
            self._chunk_wout(t0, k, tag)

    def _chunk_ln_win(self, t0, k, tag):
        sb, ps = self.sb, self.ps
        x = self.x_in.ap()
        for tt in range(NTT):
            b = tt % 2
            xt = sb["xt"][:, b, :]
            self.dma("sp", xt, x[t0 + tt * 128:t0 + (tt + 1) * 128, :],
                     w=[f"xt{b}"], tag=f"{tag}ldx{tt}")
            self._ln_tile(xt, sb["xln_tm"][:, tt, :], [f"xt{b}"], "xln_tm",
                          f"{tag}ln{tt}")
            for kc in range(NC_D):
                self.dma("sp", sb["xln_fm"][:, kc, tt * 128:(tt + 1) * 128],
                         sb["xln_tm"][:, tt, kc * 128:(kc + 1) * 128],
                         r=["xln_tm"], w=["xln_fm"], transpose=True,
                         tag=f"{tag}trx{tt}_{kc}")
        if self.dev_chunks and tag == "s0c0_":
            self.dump("xln_tm0", sb["xln_tm"], "Rhc")
        winap = self.win_bf.ap()
        for m in range(32):
            b = m % 2
            self.dma("sp", sb["win_m"][:, b],
                     winap[:, m * 128:(m + 1) * 128]
                     .rearrange("(kc p) m -> p kc m", p=128),
                     r=["win_bf"], w=[f"win{b}"], tag=f"{tag}ldw{m}")
            pk = 2 + (m % 2)
            for kc in range(NC_D):
                self.pe(lambda e, m=m, kc=kc, b=b, pk=pk: e.matmul(
                    out=ps[pk][:, 0:Q], lhsT=sb["win_m"][:, b, kc, :],
                    rhs=sb["xln_fm"][:, kc, :], start=(kc == 0),
                    stop=(kc == NC_D - 1)),
                    r=[f"win{b}", "xln_fm"], w=[f"ps{pk}"],
                    tag=f"{tag}mwin{m}_{kc}")
            if m < 16:
                dst, wkey = sb["ctx"][:, m, 3:3 + Q], "ctx"
            else:
                dst, wkey = sb["zg"][:, m - 16, :], "zg"
            self.act(lambda e, dst=dst, pk=pk: e.activation(
                out=dst, in_=ps[pk][:, 0:Q], func=AF.Identity),
                r=[f"ps{pk}"], w=[wkey], tag=f"{tag}evw{m}")

    def _chunk_conv_dt(self, k, tag):
        sb, ps = self.sb, self.ps
        if k == 0:
            self.dve(lambda e: e.memset(sb["ctx"][:, :, 0:3], 0.0),
                     w=["ctx"], tag=f"{tag}cpad")
        else:
            self.dve(lambda e: e.tensor_copy(out=sb["ctx"][:, :, 0:3],
                                             in_=sb["ctx2"]),
                     r=["ctx2"], w=["ctx"], tag=f"{tag}ctail")
        for c in range(NC_I):
            bufs = [sb["ctv"], sb["xcp"]]
            keys = ["ctv", "xcp"]
            self.dve(lambda e, c=c: e.tensor_scalar(
                out=sb["ctv"][:, c, :], in0=sb["ctx"][:, c, 0:Q],
                scalar1=sb["convw"][:, c, 0:1], scalar2=None, op0=ALU.mult),
                r=["ctx", "convw"], w=["ctv"], tag=f"{tag}cv{c}_0")
            for kk in range(1, D_CONV):
                src = bufs[(kk - 1) % 2]
                dst = bufs[kk % 2]
                self.dve(lambda e, c=c, kk=kk, src=src, dst=dst:
                         e.scalar_tensor_tensor(
                    out=dst[:, c, :], in0=sb["ctx"][:, c, kk:kk + Q],
                    scalar=sb["convw"][:, c, kk:kk + 1],
                    in1=src[:, c, :], op0=ALU.mult, op1=ALU.add),
                    r=["ctx", "convw", keys[(kk - 1) % 2]],
                    w=[keys[kk % 2]], tag=f"{tag}cv{c}_{kk}")
        self.dve(lambda e: e.tensor_copy(out=sb["ctx2"],
                                         in_=sb["ctx"][:, :, Q:Q + 3]),
                 r=["ctx"], w=["ctx2"], tag=f"{tag}ctsv")
        if self.dev_chunks and tag == "s0c0_":
            self.dump("xc0", sb["ctx"], "Rctx")
        for c in range(NC_I):
            self.dve(lambda e, c=c: e.tensor_scalar(
                out=sb["sg"][:, c, :], in0=sb["xcp"][:, c, :],
                scalar1=sb["convb"][:, c:c + 1], scalar2=None, op0=ALU.add),
                r=["xcp", "convb"], w=["sg"], tag=f"{tag}vpb{c}")
        sgf = sb["sg"].rearrange("p c q -> p (c q)")
        ctf = sb["ctv"].rearrange("p c q -> p (c q)")
        xcf = sb["xcp"].rearrange("p c q -> p (c q)")
        self.act(lambda e: e.activation(out=ctf, in_=sgf, func=AF.Exp,
                                        scale=-1.0),
                 r=["sg"], w=["ctv"], tag=f"{tag}sl_e")
        self.act(lambda e: e.activation(out=xcf, in_=ctf, func=AF.Ln,
                                        bias=sb["oner"]),
                 r=["ctv", "oner"], w=["xcp"], tag=f"{tag}sl_l")
        self.act(lambda e: e.activation(out=ctf, in_=xcf, func=AF.Exp,
                                        scale=-1.0),
                 r=["xcp"], w=["ctv"], tag=f"{tag}sl_s")
        self.dve(lambda e: e.tensor_tensor(out=xcf, in0=sgf, in1=ctf,
                                           op=ALU.mult),
                 r=["sg", "ctv"], w=["xcp"], tag=f"{tag}sl_m")
        for c in range(NC_I):
            self.pe(lambda e, c=c: e.matmul(
                out=ps[4][0:33, 0:Q], lhsT=sb["wx_lhsT"][:, c, :],
                rhs=sb["xcp"][:, c, :], start=(c == 0), stop=(c == NC_I - 1)),
                r=["wx_lhsT", "xcp"], w=["ps4"], tag=f"{tag}mwx{c}")
        self.act(lambda e: e.activation(out=sb["xdbl"], in_=ps[4][0:33, 0:Q],
                                        func=AF.Identity),
                 r=["ps4"], w=["xdbl"], tag=f"{tag}evx")
        self.pe(lambda e: e.matmul(out=ps[5][:, 0:Q],
                                   lhsT=sb["sel_bf"][:, 0:128],
                                   rhs=sb["xdbl"], start=True, stop=True),
                r=["xdbl", "sel_bf"], w=["ps5"], tag=f"{tag}bdt")
        for c in range(NC_I):
            self.act(lambda e, c=c: e.activation(
                out=sb["varg"][:, c, :], in_=ps[5][:, 0:Q], func=AF.Identity,
                scale=sb["wdt"][:, c:c + 1], bias=sb["bdt"][:, c:c + 1]),
                r=["ps5", "wdt", "bdt"], w=["varg"], tag=f"{tag}va{c}")
        vaf = sb["varg"].rearrange("p c q -> p (c q)")
        sgf2 = sb["sg"].rearrange("p c q -> p (c q)")
        self.act(lambda e: e.activation(out=sgf2, in_=vaf, func=AF.Exp),
                 r=["varg"], w=["sg"], tag=f"{tag}spe")
        self.act(lambda e: e.activation(out=vaf, in_=sgf2, func=AF.Ln,
                                        bias=sb["oner"]),
                 r=["sg", "oner"], w=["varg"], tag=f"{tag}spl")
        self.dve(lambda e: e.tensor_tensor(
            out=sb["lsgx"].rearrange("p c q -> p (c q)"),
            in0=vaf,
            in1=sb["xcp"].rearrange("p c q -> p (c q)"), op=ALU.mult),
            r=["varg", "xcp"], w=["lsgx"], tag=f"{tag}lx")
        zf = sb["zg"].rearrange("p c q -> p (c q)")
        szf = sb["szg"].rearrange("p c q -> p (c q)")
        ctf2 = sb["ctv"].rearrange("p c q -> p (c q)")
        self.act(lambda e: e.activation(out=ctf2, in_=zf, func=AF.Exp,
                                        scale=-1.0),
                 r=["zg"], w=["ctv"], tag=f"{tag}sz_e")
        self.act(lambda e: e.activation(out=szf, in_=ctf2, func=AF.Ln,
                                        bias=sb["oner"]),
                 r=["ctv", "oner"], w=["szg"], tag=f"{tag}sz_l")
        self.act(lambda e: e.activation(out=ctf2, in_=szf, func=AF.Exp,
                                        scale=-1.0),
                 r=["szg"], w=["ctv"], tag=f"{tag}sz_s")
        self.dve(lambda e: e.tensor_tensor(out=szf, in0=zf, in1=ctf2,
                                           op=ALU.mult),
                 r=["zg", "ctv"], w=["szg"], tag=f"{tag}sz_m")
        for s in range(D_STATE):
            self.pe(lambda e, s=s: e.matmul(
                out=ps[6][:, 0:Q],
                lhsT=sb["sel_bf"][:, (1 + s) * 128:(2 + s) * 128],
                rhs=sb["xdbl"], start=True, stop=True),
                r=["xdbl", "sel_bf"], w=["ps6"], tag=f"{tag}bB{s}")
            self.act(lambda e, s=s: e.activation(
                out=sb["nbc"][:, s, :], in_=ps[6][:, 0:Q], func=AF.Identity),
                r=["ps6"], w=["nbc"], tag=f"{tag}eB{s}")
            self.pe(lambda e, s=s: e.matmul(
                out=ps[7][:, 0:Q],
                lhsT=sb["sel_bf"][:, (17 + s) * 128:(18 + s) * 128],
                rhs=sb["xdbl"], start=True, stop=True),
                r=["xdbl", "sel_bf"], w=["ps7"], tag=f"{tag}bC{s}")
            self.act(lambda e, s=s: e.activation(
                out=sb["cc"][:, s, :], in_=ps[7][:, 0:Q], func=AF.Identity),
                r=["ps7"], w=["cc"], tag=f"{tag}eC{s}")

    def _chunk_scan(self, k, tag):
        sb, ps = self.sb, self.ps
        for c in range(NC_I):
            pk = c % 2
            for s in range(D_STATE):
                db = s % 2
                self.act(lambda e, c=c, s=s, db=db: e.activation(
                    out=sb["dA"][:, db, :], in_=sb["varg"][:, c, :],
                    func=AF.Exp, scale=sb["aneg"][:, c, s:s + 1]),
                    r=["varg", "aneg"], w=[f"dA{db}"], tag=f"{tag}dA{c}_{s}")
                self.dve(lambda e, c=c, s=s, db=db: e.tensor_tensor(
                    out=sb["ub"][:, db, :], in0=sb["lsgx"][:, c, :],
                    in1=sb["nbc"][:, s, :], op=ALU.mult),
                    r=["lsgx", "nbc"], w=[f"ub{db}"], tag=f"{tag}u{c}_{s}")
                self.dve(lambda e, c=c, s=s, db=db:
                         e.tensor_tensor_scan(
                    out=sb["hc"][:, s, :], data0=sb["dA"][:, db, :],
                    data1=sb["ub"][:, db, :],
                    initial=sb["state"][:, c, s:s + 1],
                    op0=ALU.mult, op1=ALU.add),
                    r=[f"dA{db}", f"ub{db}", "state"], w=["hc"],
                    tag=f"{tag}sc{c}_{s}")
                self.dve(lambda e, s=s, db=db: e.tensor_tensor(
                    out=sb["prod"][:, db, :], in0=sb["hc"][:, s, :],
                    in1=sb["cc"][:, s, :], op=ALU.mult),
                    r=["hc", "cc"], w=[f"prod{db}"],
                    tag=f"{tag}cm{c}_{s}")
                self.pe(lambda e, s=s, db=db, pk=pk: e.matmul(
                    out=ps[pk][:, 0:Q], lhsT=sb["ident_bf"],
                    rhs=sb["prod"][:, db, :], start=(s == 0),
                    stop=(s == D_STATE - 1)),
                    r=[f"prod{db}", "ident_bf"], w=[f"ps{pk}"],
                    tag=f"{tag}ac{c}_{s}")
                if self.dev_chunks and c == NC_I - 1 and tag == "s0c0_":
                    dsn = sb["wg_t0"].rearrange("p (a b) -> p a b", a=16)
                    usn = sb["wg_t1"].rearrange("p (a b) -> p a b", a=16)
                    self.dve(lambda e, s=s, db=db, dsn=dsn: e.tensor_copy(
                        out=dsn[:, s, :], in_=sb["dA"][:, db, :]),
                        r=[f"dA{db}"], w=["wg_t0"], tag=f"{tag}snd{s}")
                    self.dve(lambda e, s=s, db=db, usn=usn: e.tensor_copy(
                        out=usn[:, s, :], in_=sb["ub"][:, db, :]),
                        r=[f"ub{db}"], w=["wg_t1"], tag=f"{tag}snu{s}")
            self.dve(lambda e, c=c: e.tensor_copy(
                out=sb["state"][:, c, :],
                in_=sb["hc"][:, :, Q - 1:Q].squeeze(2)),
                r=["hc"], w=["state"], tag=f"{tag}st{c}")
            self.dve(lambda e, c=c, pk=pk: e.scalar_tensor_tensor(
                out=sb["prod"][:, 0, :], in0=sb["xcp"][:, c, :],
                scalar=sb["dparam"][:, c:c + 1], in1=ps[pk][:, 0:Q],
                op0=ALU.mult, op1=ALU.add),
                r=["xcp", "dparam", f"ps{pk}"], w=["prod0"],
                tag=f"{tag}g1_{c}")
            self.dve(lambda e, c=c: e.tensor_tensor(
                out=sb["ybf"][:, c, :], in0=sb["prod"][:, 0, :],
                in1=sb["szg"][:, c, :], op=ALU.mult),
                r=["prod0", "szg"], w=["ybf"], tag=f"{tag}g2_{c}")

    def _chunk_wout(self, t0, k, tag):
        sb, ps = self.sb, self.ps
        woap = self.wout_bf.ap()
        for kc in range(NC_I):
            b = kc % 2
            self.dma("sp", sb["wout_k"][:, b, :],
                     woap[kc * 128:(kc + 1) * 128, :],
                     r=["wout_bf"], w=[f"wo{b}"], tag=f"{tag}lwo{kc}")
            for tt in range(NTT):
                for n in range(2):
                    pk = tt * 2 + n
                    self.pe(lambda e, kc=kc, tt=tt, n=n, b=b, pk=pk: e.matmul(
                        out=ps[pk][:, 0:512],
                        lhsT=sb["ybf"][:, kc, tt * 128:(tt + 1) * 128],
                        rhs=sb["wout_k"][:, b, n * 512:(n + 1) * 512],
                        start=(kc == 0), stop=(kc == NC_I - 1)),
                        r=[f"wo{b}", "ybf"], w=[f"ps{pk}"],
                        tag=f"{tag}mo{kc}_{tt}_{n}")
        x = self.x_in.ap()
        for tt in range(NTT):
            b = tt % 2
            xt = sb["xt"][:, b, :]
            rowsl = slice(t0 + tt * 128, t0 + (tt + 1) * 128)
            self.dma("sp", xt, x[rowsl, :], w=[f"xt{b}"], tag=f"{tag}rlx{tt}")
            x1 = sb["x1t"][:, b, :]
            for n in range(2):
                pk = tt * 2 + n
                self.dve(lambda e, xt=xt, x1=x1, n=n, pk=pk: e.tensor_tensor(
                    out=x1[:, n * 512:(n + 1) * 512],
                    in0=ps[pk][:, 0:512],
                    in1=xt[:, n * 512:(n + 1) * 512], op=ALU.add),
                    r=[f"ps{pk}", f"xt{b}"], w=[f"x1t{b}"],
                    tag=f"{tag}res{tt}_{n}")
            self.dma("sp", self.x1_hbm.ap()[rowsl, :], x1, r=[f"x1t{b}"],
                     w=["x1_hbm"], tag=f"{tag}sx1_{tt}")
            x1l = sb["x1lt"][:, b, :]
            self._ln_tile(x1, x1l, [f"x1t{b}"], f"x1lt{b}", f"{tag}ln2_{tt}")
            self.dma("sp", self.x1ln_hbm.ap()[rowsl, :], x1l,
                     r=[f"x1lt{b}"], w=["x1ln_hbm"], tag=f"{tag}sxl_{tt}")

    def build_ffn(self):
        import os
        stages = int(os.environ.get("FFN_STAGES", "5"))
        sb = self.sb
        dev = self.debug_phase_a and self.dev_full
        self._ffn_routing()
        if stages < 2:
            if dev:
                self.dump_raw("cw", sb["cw"], "xt0", F32)
                self.dump_raw("eki", sb["eki"], "xt0", F32)
            return
        if dev:
            self.dump_raw("cw", sb["cw"], "xt0", F32)
            self.dump_raw("eki", sb["eki"], "xt0", F32)
            self.dump_raw("biase", sb["bias_e"], "bias_e", F32)
            self.dump_raw("iota16d", sb["iota16"], "iota16", F32)
            self.dump_raw("io999d", sb["io999"], "io999", F32)
            self.dump_raw("srtf", sb["srt"], "xt0", F32)
            self.dump_raw("routr", sb["rout_rhs"], "rout_rhs", BF16)
        self._ffn_schedule()
        if stages < 3:
            return
        if dev:
            self.dump_raw("flags", sb["flags"], "Rctx", F32)
            self.dump_raw("posq", sb["pos"][:, 0:512], "Rctv", F32)
            self.dump_raw("cnt", sb["cnt"], "sqd", F32)
            self.dump_raw("eidr", sb["eidr"], "sqd", F32)
            self.dump_raw("widx", sb["widx"], "sqd", I32)
        self._ffn_scatter()
        if stages < 4:
            return
        if dev:
            self.dump_raw("dsti", sb["dsti"], "xt0", I32)
        self._ffn_blocks()
        if stages < 5:
            return
        self._ffn_shared_combine()

    def _ffn_routing(self):
        sb = self.sb
        dev = self.debug_phase_a and self.dev_full
        if dev:
            self.dbg_scall = self.nc.dram_tensor(
                "dbg_scall", [TOK // 128, 128, E], F32, kind="ExternalOutput")
        NTOK_T = TOK // 128
        for tt in range(NTOK_T):
            b = tt % 2
            xl = sb["x1lt"][:, b, :]
            self.dma("sp", xl, self.x1ln_hbm.ap()[tt * 128:(tt + 1) * 128, :],
                     r=["x1ln_hbm"], w=[f"x1lt{b}"], tag=f"r{tt}_ld")
            self._ffn_routing_tile(tt)

    def _ffn_routing_tile(self, tt):
        sb, ps = self.sb, self.ps
        b = tt % 2
        tg = f"r{tt}_"
        xl = sb["x1lt"][:, b, :]
        self.tr_fenced(
            NC_D,
            lambda kc: (xl[:, kc * 128:(kc + 1) * 128], sb["ident_bf"],
                        [f"x1lt{b}", "ident_bf"]),
            lambda kc: (ps[2 + kc % 2][:, 0:128], f"ps{2 + kc % 2}"),
            lambda kc, psap, pskey, fence: self.act(
                lambda e, kc=kc, psap=psap: e.activation(
                    out=sb["rfm"][:, kc, :], in_=psap, func=AF.Identity),
                r=[pskey, fence], w=["rfm"], tag=f"{tg}te{kc}"),
            f"{tg}RF")
        for kc in range(NC_D):
            self.pe(lambda e, kc=kc: e.matmul(
                out=ps[0][:, 0:E], lhsT=sb["rfm"][:, kc, :],
                rhs=sb["rout_rhs"][:, kc, :], start=(kc == 0),
                stop=(kc == NC_D - 1)),
                r=["rfm", "rout_rhs"], w=["ps0"], tag=f"{tg}mm{kc}")
        self.act(lambda e: e.activation(out=sb["scores"],
                                        in_=ps[0][:, 0:E],
                                        func=AF.Exp, scale=-1.0),
                 r=["ps0"], w=["scores"], tag=f"{tg}sg1")
        self.act(lambda e: e.activation(out=sb["srt"][:, 7, :],
                                        in_=sb["scores"],
                                        func=AF.Ln, bias=sb["oner"]),
                 r=["scores", "oner"], w=["srt"], tag=f"{tg}sg2")
        self.act(lambda e: e.activation(out=sb["scores"],
                                        in_=sb["srt"][:, 7, :],
                                        func=AF.Exp, scale=-1.0),
                 r=["srt"], w=["scores"], tag=f"{tg}sg3")
        self._routing_math(tt)

    def _routing_math(self, tt):
        sb, ps = self.sb, self.ps
        tg = f"r{tt}_"
        S = sb["srt"]
        d = self.dve
        d(lambda e: e.tensor_tensor(out=S[:, 0, :], in0=sb["scores"],
                                    in1=sb["bias_e"], op=ALU.add),
          r=["scores", "bias_e"], w=["srt"], tag=f"{tg}s")
        sgv = S[:, 0, :].rearrange("p (g e) -> p g e", g=NG)

        def tt2(o, a, bb, op, rk=("srt",), tag2=""):
            d(lambda e, o=o, a=a, bb=bb, op=op: e.tensor_tensor(
                out=o, in0=a, in1=bb, op=op),
              r=list(rk), w=["srt"], tag=f"{tg}{tag2}")
        v = [sgv[:, :, j] for j in range(4)]
        tt2(S[:, 1, 0:4], v[0], v[1], ALU.max, tag2="m1")
        tt2(S[:, 1, 4:8], v[0], v[1], ALU.min, tag2="n1")
        tt2(S[:, 1, 8:12], v[2], v[3], ALU.max, tag2="m2")
        tt2(S[:, 1, 12:16], v[2], v[3], ALU.min, tag2="n2")
        tt2(S[:, 2, 0:4], S[:, 1, 0:4], S[:, 1, 8:12], ALU.max, tag2="t1")
        tt2(S[:, 2, 4:8], S[:, 1, 0:4], S[:, 1, 8:12], ALU.min, tag2="ta")
        tt2(S[:, 2, 8:12], S[:, 1, 4:8], S[:, 1, 12:16], ALU.max,
            tag2="tb")
        tt2(S[:, 2, 12:16], S[:, 2, 4:8], S[:, 2, 8:12], ALU.max,
            tag2="sec")
        tt2(S[:, 3, 0:4], S[:, 2, 0:4], S[:, 2, 12:16], ALU.add,
            tag2="gsc")
        g = [S[:, 3, j:j + 1] for j in range(4)]
        tt2(S[:, 3, 4:5], g[0], g[1], ALU.max, tag2="M1")
        tt2(S[:, 3, 5:6], g[0], g[1], ALU.min, tag2="N1")
        tt2(S[:, 3, 6:7], g[2], g[3], ALU.max, tag2="M2")
        tt2(S[:, 3, 7:8], g[2], g[3], ALU.min, tag2="N2")
        tt2(S[:, 3, 8:9], S[:, 3, 4:5], S[:, 3, 6:7], ALU.min, tag2="ga")
        tt2(S[:, 3, 9:10], S[:, 3, 5:6], S[:, 3, 7:8], ALU.max, tag2="gb")
        tt2(S[:, 3, 10:11], S[:, 3, 8:9], S[:, 3, 9:10], ALU.max,
            tag2="thr")
        d(lambda e: e.tensor_tensor(out=S[:, 4, 0:4], in0=S[:, 3, 0:4],
                                    in1=S[:, 3, 10:11].broadcast_to([128, 4]),
                                    op=ALU.is_ge),
          r=["srt"], w=["srt"], tag=f"{tg}gm")
        gmbc = S[:, 4, 0:4].unsqueeze(2).broadcast_to([128, 4, 4])
        d(lambda e: e.tensor_scalar(out=S[:, 5, :], in0=S[:, 0, :],
                                    scalar1=1.0, scalar2=None,
                                    op0=ALU.add),
          r=["srt"], w=["srt"], tag=f"{tg}sp1")
        d(lambda e: e.tensor_tensor(
            out=S[:, 1, :].rearrange("p (g e) -> p g e", g=NG),
            in0=gmbc,
            in1=S[:, 5, :].rearrange("p (g e) -> p g e", g=NG),
            op=ALU.mult),
          r=["srt"], w=["srt"], tag=f"{tg}em")
        d(lambda e: e.tensor_scalar(out=S[:, 1, :], in0=S[:, 1, :],
                                    scalar1=-1.0, scalar2=None,
                                    op0=ALU.add),
          r=["srt"], w=["srt"], tag=f"{tg}mk")
        mrow = [1, 3]
        frow = [6, 0]
        for j in range(TOPK):
            mi = S[:, mrow[j % 2], :]
            mo = S[:, mrow[(j + 1) % 2], :]
            d(lambda e, mi=mi: e.tensor_reduce(out=S[:, 2, 0:1], in_=mi,
                                        axis=AX, op=ALU.max),
              r=["srt"], w=["srt"], tag=f"{tg}mx{j}")
            d(lambda e, mi=mi: e.tensor_tensor(out=S[:, 5, :], in0=mi,
                                        in1=S[:, 2, 0:1]
                                        .broadcast_to([128, 16]),
                                        op=ALU.is_equal),
              r=["srt"], w=["srt"], tag=f"{tg}eq{j}")
            d(lambda e: e.tensor_tensor(out=S[:, 4, :], in0=S[:, 5, :],
                                        in1=sb["io999"], op=ALU.mult),
              r=["srt", "io999"], w=["srt"], tag=f"{tg}cn{j}")
            d(lambda e: e.tensor_scalar(out=S[:, 7, :], in0=S[:, 4, :],
                                        scalar1=-1.0, scalar2=999.0,
                                        op0=ALU.mult, op1=ALU.add),
              r=["srt"], w=["srt"], tag=f"{tg}cf{j}")
            d(lambda e, j=j: e.tensor_reduce(out=sb["eki"][:, tt, j:j + 1],
                                             in_=S[:, 7, :], axis=AX,
                                             op=ALU.min),
              r=["srt"], w=["eki"], tag=f"{tg}ix{j}")
            d(lambda e, j=j: e.tensor_tensor(
                out=S[:, 5, :], in0=sb["iota16"],
                in1=sb["eki"][:, tt, j:j + 1].broadcast_to([128, 16]),
                op=ALU.is_equal),
              r=["eki", "iota16"], w=["srt"], tag=f"{tg}sl{j}")
            d(lambda e: e.tensor_tensor(out=S[:, 4, :], in0=S[:, 5, :],
                                        in1=sb["scores"], op=ALU.mult),
              r=["srt", "scores"], w=["srt"], tag=f"{tg}tw{j}")
            d(lambda e, j=j: e.tensor_reduce(out=S[:, 2, 2 + j:3 + j],
                                             in_=S[:, 4, :], axis=AX,
                                             op=ALU.add),
              r=["srt"], w=["srt"], tag=f"{tg}ts{j}")
            d(lambda e, mi=mi, mo=mo: e.scalar_tensor_tensor(
                out=mo, in0=S[:, 5, :], scalar=-1e9,
                in1=mi, op0=ALU.mult, op1=ALU.add),
              r=["srt"], w=["srt"], tag=f"{tg}up{j}")
            if j == 0:
                d(lambda e: e.tensor_copy(out=S[:, 6, :], in_=S[:, 5, :]),
                  r=["srt"], w=["srt"], tag=f"{tg}f0")
            else:
                fi = S[:, frow[(j + 1) % 2], :]
                fo = S[:, frow[j % 2], :]
                d(lambda e, fi=fi, fo=fo: e.tensor_tensor(out=fo,
                                            in0=fi,
                                            in1=S[:, 5, :], op=ALU.add),
                  r=["srt"], w=["srt"], tag=f"{tg}fa{j}")
        d(lambda e: e.tensor_reduce(out=S[:, 2, 6:7], in_=S[:, 2, 2:6],
                                    axis=AX, op=ALU.add),
          r=["srt"], w=["srt"], tag=f"{tg}tsm")
        d(lambda e: e.tensor_scalar(out=S[:, 2, 7:8], in0=S[:, 2, 6:7],
                                    scalar1=1e-20, scalar2=None,
                                    op0=ALU.add),
          r=["srt"], w=["srt"], tag=f"{tg}te")
        d(lambda e: e.reciprocal(out=S[:, 3, 0:1], in_=S[:, 2, 7:8]),
          r=["srt"], w=["srt"], tag=f"{tg}rc")
        d(lambda e: e.tensor_scalar(out=S[:, 3, 4:5], in0=S[:, 3, 0:1],
                                    scalar1=MOE_SCALE, scalar2=None,
                                    op0=ALU.mult),
          r=["srt"], w=["srt"], tag=f"{tg}rc2")
        d(lambda e: e.tensor_tensor(out=sb["cw"][:, tt, :],
                                    in0=S[:, 2, 2:6],
                                    in1=S[:, 3, 4:5].broadcast_to([128, 4]),
                                    op=ALU.mult),
          r=["srt"], w=["cw"], tag=f"{tg}cw")
        self.tr_fenced(
            1,
            lambda kc: (S[:, frow[(TOPK - 1) % 2], :], sb["ident_f32"],
                        ["srt", "ident_f32"]),
            lambda kc: ((ps[1][0:16, 0:128] if kc % 2 == 0 else
                         ps[2][0:16, 0:128]),
                        f"ps{1 + kc % 2}"),
            lambda kc, psap, pskey, fence: self.act(
                lambda e, psap=psap, tt=tt: e.activation(
                    out=sb["flags"][:, tt * 128:(tt + 1) * 128],
                    in_=psap, func=AF.Identity),
                r=[pskey, fence], w=["flags"], tag=f"{tg}evf"),
            f"{tg}TF")

    def _ffn_schedule(self):
        sb, ps = self.sb, self.ps
        d = self.dve
        d(lambda e: e.memset(sb["ones16k"], 1.0), w=["ones16k"], tag="fone")
        d(lambda e: e.tensor_tensor_scan(
            out=sb["pos"], data0=sb["ones16k"], data1=sb["flags"],
            initial=0.0, op0=ALU.mult, op1=ALU.add),
          r=["ones16k", "flags"], w=["pos"], tag="fpos")
        C = sb["cnt"]
        d(lambda e: e.tensor_copy(out=C[:, 0:1], in_=sb["pos"][:, TOK - 1:TOK]),
          r=["pos"], w=["cnt"], tag="fcnt")
        d(lambda e: e.tensor_scalar(out=C[:, 1:2], in0=C[:, 0:1],
                                    scalar1=FB / 2 - 0.5, scalar2=1.0 / FB,
                                    op0=ALU.add, op1=ALU.mult),
          r=["cnt"], w=["cnt"], tag="fblk")
        cnti = sb["sqd"].bitcast(I32)[0:16, 300:301]
        d(lambda e: e.tensor_copy(out=cnti, in_=C[:, 1:2]),
          r=["cnt"], w=["cnt"], tag="ftrunc")
        d(lambda e: e.tensor_copy(out=C[:, 3:4], in_=cnti),
          r=["cnt"], w=["cnt"], tag="fback")
        d(lambda e: e.tensor_scalar(out=C[:, 4:5], in0=C[:, 3:4],
                                    scalar1=float(FB), scalar2=None,
                                    op0=ALU.mult),
          r=["cnt"], w=["cnt"], tag="fpad")
        self.pe(lambda e: e.matmul(out=ps[2][0:16, 0:1], lhsT=sb["ltri16"],
                                   rhs=C[:, 4:5], start=True, stop=True),
                r=["cnt", "ltri16"], w=["ps2"], tag="fbase")
        self.act(lambda e: e.activation(out=C[:, 5:6], in_=ps[2][0:16, 0:1],
                                        func=AF.Identity),
                 r=["ps2"], w=["cnt"], tag="fevb")
        self.pe(lambda e: e.matmul(out=ps[3][0:16, 0:1], lhsT=sb["ltri16i"],
                                   rhs=C[:, 3:4], start=True, stop=True),
                r=["cnt", "ltri16i"], w=["ps3"], tag="fcumb")
        self.act(lambda e: e.activation(out=C[:, 6:7], in_=ps[3][0:16, 0:1],
                                        func=AF.Identity),
                 r=["ps3"], w=["cnt"], tag="fevc")
        self.tr_fenced(
            2,
            lambda kc: (C[:, 6 - kc:7 - kc], sb["ident_f32"][0:16, 0:16],
                        ["cnt", "ident_f32"]),
            lambda kc: (ps[4 + kc % 2][0:1, 0:16], f"ps{4 + kc % 2}"),
            lambda kc, psap, pskey, fence: self.act(
                lambda e, psap=psap, kc=kc: e.activation(
                    out=(sb["cumbr"] if kc == 0 else sb["baser"]),
                    in_=psap, func=AF.Identity),
                r=[pskey, fence], w=["sqd"],
                tag=("fevcr" if kc == 0 else "fevbr")),
            "FC")
        eidt = sb["sqd"][0:1, 128:128 + NB]
        eidb = sb["sqd"][0:1, 832:832 + NB]
        acc = [sb["eidr"], eidb]
        d(lambda e: e.memset(sb["eidr"], 0.0), w=["sqd"], tag="fez")
        for ei in range(E):
            d(lambda e, ei=ei: e.tensor_tensor(
                out=eidt,
                in0=sb["cumbr"][0:1, ei:ei + 1].broadcast_to([1, NB]),
                in1=sb["iota_nb"], op=ALU.is_le),
              r=["sqd", "iota_nb"], w=["sqd"], tag=f"fel{ei}")
            src = acc[ei % 2]
            dst = acc[(ei + 1) % 2]
            d(lambda e, src=src, dst=dst: e.tensor_tensor(
                out=dst, in0=src, in1=eidt, op=ALU.add),
              r=["sqd"], w=["sqd"], tag=f"fea{ei}")
        self.pe(lambda e: e.matmul(out=ps[6][:, 0:NB], lhsT=sb["ones_row"],
                                   rhs=sb["eidr"], start=True, stop=True),
                r=["sqd", "ones_row"], w=["ps6"], tag="fbw")
        d(lambda e: e.tensor_scalar(out=sb["widx"], in0=ps[6][:, 0:NB],
                                    scalar1=128.0, scalar2=sb["iota_p"],
                                    op0=ALU.mult, op1=ALU.add),
          r=["ps6", "iota_p"], w=["sqd"], tag="fwidx")
        self.pe(lambda e: e.matmul(out=ps[7][:, 0:16], lhsT=sb["ones_row"],
                                   rhs=sb["baser"], start=True, stop=True),
                r=["sqd", "ones_row"], w=["ps7"], tag="fbb")
        self.act(lambda e: e.activation(out=sb["base_bc"], in_=ps[7][:, 0:16],
                                        func=AF.Identity),
                 r=["ps7"], w=["sqd"], tag="fevbb")
        self.tr_fenced(
            TOK // 128,
            lambda tt: (sb["pos"][:, tt * 128:(tt + 1) * 128],
                        sb["ident_f32"][0:16, 0:16], ["pos", "ident_f32"]),
            lambda tt: (ps[tt % 2][0:128, 0:16], f"ps{tt % 2}"),
            lambda tt, psap, pskey, fence: self.act(
                lambda e, tt=tt, psap=psap: e.activation(
                    out=sb["post"][:, tt, :], in_=psap, func=AF.Identity),
                r=[pskey, fence], w=["post"], tag=f"fpe{tt}"),
            "FP")

    def _ffn_scatter(self):
        sb = self.sb
        d = self.dve
        S = sb["srt"]
        for tt in range(TOK // 128):
            b = tt % 2
            tg = f"sc{tt}_"
            xl = sb["x1lt"][:, b, :]
            self.dma("sp", xl, self.x1ln_hbm.ap()[tt * 128:(tt + 1) * 128, :],
                     r=["x1ln_hbm"], w=[f"x1lt{b}"], tag=f"{tg}ld")
            for j in range(TOPK):
                d(lambda e, tt=tt, j=j: e.tensor_tensor(
                    out=S[:, 5, :], in0=sb["iota16"],
                    in1=sb["eki"][:, tt, j:j + 1].broadcast_to([128, 16]),
                    op=ALU.is_equal),
                  r=["eki", "iota16"], w=["srt"], tag=f"{tg}sl{j}")
                d(lambda e, tt=tt: e.tensor_tensor(
                    out=S[:, 4, :], in0=S[:, 5, :], in1=sb["post"][:, tt, :],
                    op=ALU.mult),
                  r=["srt", "post"], w=["srt"], tag=f"{tg}pm{j}")
                d(lambda e: e.tensor_reduce(out=S[:, 2, 0:1], in_=S[:, 4, :],
                                            axis=AX, op=ALU.add),
                  r=["srt"], w=["srt"], tag=f"{tg}pr{j}")
                d(lambda e: e.tensor_tensor(out=S[:, 4, :], in0=S[:, 5, :],
                                            in1=sb["base_bc"], op=ALU.mult),
                  r=["srt", "sqd"], w=["srt"], tag=f"{tg}bm{j}")
                d(lambda e: e.tensor_reduce(out=S[:, 2, 1:2], in_=S[:, 4, :],
                                            axis=AX, op=ALU.add),
                  r=["srt"], w=["srt"], tag=f"{tg}br{j}")
                d(lambda e: e.tensor_tensor(out=S[:, 2, 2:3],
                                            in0=S[:, 2, 0:1],
                                            in1=S[:, 2, 1:2], op=ALU.add),
                  r=["srt"], w=["srt"], tag=f"{tg}ds{j}")
                d(lambda e, tt=tt, j=j: e.tensor_scalar(
                    out=sb["dsti"][:, tt, j:j + 1], in0=S[:, 2, 2:3],
                    scalar1=-1.0, scalar2=None, op0=ALU.add),
                  r=["srt"], w=["dsti"], tag=f"{tg}di{j}")
                self.pool(lambda e, tt=tt, j=j, xl=xl: e.indirect_dma_start(
                    out=self.xg_hbm.ap(),
                    out_offset=IndirectOffsetOnAxis(
                        ap=sb["dsti"][:, tt, j:j + 1], axis=0),
                    in_=xl, in_offset=None),
                    r=["dsti", f"x1lt{b}"], w=["xg_hbm"], dma=True,
                    tag=f"{tg}sx{j}")

    def _ffn_fetch(self, bi, buf):
        sb = self.sb
        for nm, tab in [("wg_t", self.gate_tab), ("wu_t", self.up_tab),
                        ("wd_t", self.down_tab)]:
            self.pool(lambda e, nm=nm, tab=tab, bi=bi, buf=buf:
                      e.indirect_dma_start(
                out=sb[f"{nm}{buf}"], out_offset=None,
                in_=tab.ap(),
                in_offset=IndirectOffsetOnAxis(
                    ap=sb["widx"][:, bi:bi + 1], axis=0)),
                r=["sqd", tab.name], w=[f"{nm}{buf}"], dma=True,
                tag=f"fw{bi}_{nm}")

    def _ffn_blocks(self):
        sb, ps = self.sb, self.ps
        NT4 = FB // 128
        self._ffn_fetch(0, 0)
        self._ffn_loadx(0, 0)
        for b in range(NB):
            buf = b % 2
            tg = f"b{b}_"
            if b + 1 < NB:
                self._ffn_fetch(b + 1, (b + 1) % 2)
                self._ffn_loadx(b + 1, (b + 1) % 2)
            for mc in range(NC_D):
                pg = mc % 2
                pu = 2 + mc % 2
                for kc in range(NC_D):
                    self.pe(lambda e, mc=mc, kc=kc, buf=buf, pg=pg: e.matmul(
                        out=ps[pg][:, 0:FB],
                        lhsT=sb[f"wg_t{buf}"][:,
                                        kc * 1024 + mc * 128:
                                        kc * 1024 + mc * 128 + 128],
                        rhs=sb[f"xg_fm{buf}"][:, kc, :], start=(kc == 0),
                        stop=(kc == NC_D - 1)),
                        r=[f"wg_t{buf}", f"xg_fm{buf}"], w=[f"ps{pg}"],
                        tag=f"{tg}g{mc}_{kc}")
                self.act(lambda e, mc=mc, pg=pg: e.activation(
                    out=sb["eg"][:, 0, :], in_=ps[pg][:, 0:FB],
                    func=AF.Exp, scale=-1.0),
                    r=[f"ps{pg}"], w=["eg"], tag=f"{tg}sa{mc}")
                self.act(lambda e: e.activation(
                    out=sb["tmu"][:, 0, :], in_=sb["eg"][:, 0, :],
                    func=AF.Ln, bias=sb["oner"]),
                    r=["eg", "oner"], w=["tmu"], tag=f"{tg}sb{mc}")
                self.act(lambda e, mc=mc: e.activation(
                    out=sb["sgate"][:, mc, :], in_=sb["tmu"][:, 0, :],
                    func=AF.Exp, scale=-1.0),
                    r=["tmu"], w=["sgate"], tag=f"{tg}sc{mc}")
                for kc in range(NC_D):
                    self.pe(lambda e, mc=mc, kc=kc, buf=buf, pu=pu: e.matmul(
                        out=ps[pu][:, 0:FB],
                        lhsT=sb[f"wu_t{buf}"][:,
                                        kc * 1024 + mc * 128:
                                        kc * 1024 + mc * 128 + 128],
                        rhs=sb[f"xg_fm{buf}"][:, kc, :], start=(kc == 0),
                        stop=(kc == NC_D - 1)),
                        r=[f"wu_t{buf}", f"xg_fm{buf}"], w=[f"ps{pu}"],
                        tag=f"{tg}u{mc}_{kc}")
                self.dve(lambda e, mc=mc, pu=pu: e.tensor_tensor(
                    out=sb["tmu"][:, 0, :], in0=sb["sgate"][:, mc, :],
                    in1=ps[pu][:, 0:FB], op=ALU.mult),
                    r=["sgate", f"ps{pu}"], w=["tmu"], tag=f"{tg}h{mc}")
                self.dve(lambda e, mc=mc, pg=pg: e.tensor_tensor(
                    out=sb["hub"][:, mc, :], in0=sb["tmu"][:, 0, :],
                    in1=ps[pg][:, 0:FB], op=ALU.mult),
                    r=["tmu", f"ps{pg}"], w=["hub"], tag=f"{tg}h2{mc}")
            for t4 in range(NT4):
                for n in range(2):
                    pd = 4 + (t4 * 2 + n) % 4
                    for kc in range(NC_D):
                        self.pe(lambda e, t4=t4, n=n, kc=kc, buf=buf, pd=pd:
                                e.matmul(
                            out=ps[pd][:, 0:512],
                            lhsT=sb["hub"][:, kc, t4 * 128:(t4 + 1) * 128],
                            rhs=sb[f"wd_t{buf}"][:,
                                           kc * 1024 + n * 512:
                                           kc * 1024 + n * 512 + 512],
                            start=(kc == 0), stop=(kc == NC_D - 1)),
                            r=["hub", f"wd_t{buf}"], w=[f"ps{pd}"],
                            tag=f"{tg}d{t4}_{n}_{kc}")
                    self.act(lambda e, t4=t4, n=n, buf=buf, pd=pd:
                             e.activation(
                        out=sb[f"ygb{buf}"][:, t4, n * 512:(n + 1) * 512],
                        in_=ps[pd][:, 0:512], func=AF.Identity),
                        r=[f"ps{pd}"], w=[f"ygb{buf}"], tag=f"{tg}ev{t4}_{n}")
                self.dma("sp", self.yg_hbm.ap()[b * FB + t4 * 128:
                                                b * FB + (t4 + 1) * 128, :],
                         sb[f"ygb{buf}"][:, t4, :],
                         r=[f"ygb{buf}"], w=["yg_hbm"], tag=f"{tg}st{t4}")

    def _ffn_loadx(self, bi, buf):
        sb = self.sb
        for t4 in range(FB // 128):
            b2 = t4 % 2
            xl = sb["x1lt"][:, b2, :]
            self.dma("sp", xl,
                     self.xg_hbm.ap()[bi * FB + t4 * 128:
                                      bi * FB + (t4 + 1) * 128, :],
                     r=["xg_hbm"], w=[f"x1lt{b2}"], tag=f"lx{bi}_{t4}")
            for kc in range(NC_D):
                self.dma("sp", sb[f"xg_fm{buf}"][:, kc,
                                           t4 * 128:(t4 + 1) * 128],
                         xl[:, kc * 128:(kc + 1) * 128],
                         r=[f"x1lt{b2}"], w=[f"xg_fm{buf}"], transpose=True,
                         tag=f"lt{bi}_{t4}_{kc}")

    def _ffn_shared_combine(self):
        sb, ps = self.sb, self.ps
        self.dma("sp", sb["shgw"],
                 self.shg_bf.ap().rearrange("(kc p) m -> p kc m", p=128),
                 r=["shg_bf"], w=["shgw"], tag="lshg")
        self.dma("sp", sb["shuw"],
                 self.shu_bf.ap().rearrange("(kc p) m -> p kc m", p=128),
                 r=["shu_bf"], w=["shuw"], tag="lshu")
        self.dma("sp", sb["shdw"],
                 self.shd_bf.ap().rearrange("(kc p) m -> p kc m", p=128),
                 r=["shd_bf"], w=["shdw"], tag="lshd")
        NCH = TOK // FB
        for ch in range(NCH):
            tg = f"h{ch}_"
            for t4 in range(NTT):
                b2 = t4 % 2
                tt = ch * NTT + t4
                xl = sb["x1lt"][:, b2, :]
                self.dma("sp", xl,
                         self.x1ln_hbm.ap()[tt * 128:(tt + 1) * 128, :],
                         r=["x1ln_hbm"], w=[f"x1lt{b2}"], tag=f"{tg}lx{t4}")
                for kc in range(NC_D):
                    self.dma("sp", sb["sxfm"][:, kc, t4 * 128:(t4 + 1) * 128],
                             xl[:, kc * 128:(kc + 1) * 128],
                             r=[f"x1lt{b2}"], w=["sxfm"], transpose=True,
                             tag=f"{tg}tx{t4}_{kc}")
            for mc in range(NC_I):
                pg = mc % 2
                pu = 2 + mc % 2
                for kc in range(NC_D):
                    self.pe(lambda e, mc=mc, kc=kc, pg=pg: e.matmul(
                        out=ps[pg][:, 0:FB],
                        lhsT=sb["shgw"][:, kc, mc * 128:(mc + 1) * 128],
                        rhs=sb["sxfm"][:, kc, :], start=(kc == 0),
                        stop=(kc == NC_D - 1)),
                        r=["shgw", "sxfm"], w=[f"ps{pg}"],
                        tag=f"{tg}g{mc}_{kc}")
                self.act(lambda e, mc=mc, pg=pg: e.activation(
                    out=sb["tsil"][:, 0, :], in_=ps[pg][:, 0:FB],
                    func=AF.Exp, scale=-1.0),
                    r=[f"ps{pg}"], w=["tsil"], tag=f"{tg}sa{mc}")
                self.act(lambda e: e.activation(
                    out=sb["eg4"][:, 0, :], in_=sb["tsil"][:, 0, :],
                    func=AF.Ln, bias=sb["oner"]),
                    r=["tsil", "oner"], w=["eg4"], tag=f"{tg}sb{mc}")
                self.act(lambda e: e.activation(
                    out=sb["tsil"][:, 0, :], in_=sb["eg4"][:, 0, :],
                    func=AF.Exp, scale=-1.0),
                    r=["eg4"], w=["tsil"], tag=f"{tg}sc{mc}")
                for kc in range(NC_D):
                    self.pe(lambda e, mc=mc, kc=kc, pu=pu: e.matmul(
                        out=ps[pu][:, 0:FB],
                        lhsT=sb["shuw"][:, kc, mc * 128:(mc + 1) * 128],
                        rhs=sb["sxfm"][:, kc, :], start=(kc == 0),
                        stop=(kc == NC_D - 1)),
                        r=["shuw", "sxfm"], w=[f"ps{pu}"],
                        tag=f"{tg}u{mc}_{kc}")
                self.dve(lambda e, mc=mc, pu=pu: e.tensor_tensor(
                    out=sb["eg4"][:, 0, :], in0=sb["tsil"][:, 0, :],
                    in1=ps[pu][:, 0:FB], op=ALU.mult),
                    r=["tsil", f"ps{pu}"], w=["eg4"], tag=f"{tg}h{mc}")
                self.dve(lambda e, mc=mc, pg=pg: e.tensor_tensor(
                    out=sb["shhub"][:, mc, :], in0=sb["eg4"][:, 0, :],
                    in1=ps[pg][:, 0:FB], op=ALU.mult),
                    r=["eg4", f"ps{pg}"], w=["shhub"], tag=f"{tg}h2{mc}")
            for t4 in range(NTT):
                b2 = t4 % 2
                tt = ch * NTT + t4
                x1 = sb["x1t"][:, b2, :]
                self.dma("sp", x1,
                         self.x1_hbm.ap()[tt * 128:(tt + 1) * 128, :],
                         r=["x1_hbm"], w=[f"x1t{b2}"], tag=f"{tg}l1{t4}")
                for j in range(TOPK):
                    self.pool(lambda e, tt=tt, j=j, b2=b2:
                              e.indirect_dma_start(
                        out=sb[f"gat{b2}"][:, j, :], out_offset=None,
                        in_=self.yg_hbm.ap(),
                        in_offset=IndirectOffsetOnAxis(
                            ap=sb["dsti"][:, tt, j:j + 1], axis=0)),
                        r=["dsti", "yg_hbm"], w=[f"gat{b2}"], dma=True,
                        tag=f"{tg}gy{t4}_{j}")
                ot = sb["ot"]
                ot2 = sb["ot2"]
                for n in range(2):
                    pd = 4 + (t4 * 2 + n) % 4
                    for kc in range(NC_I):
                        self.pe(lambda e, t4=t4, n=n, kc=kc, pd=pd: e.matmul(
                            out=ps[pd][:, 0:512],
                            lhsT=sb["shhub"][:, kc, t4 * 128:(t4 + 1) * 128],
                            rhs=sb["shdw"][:, kc, n * 512:(n + 1) * 512],
                            start=(kc == 0), stop=(kc == NC_I - 1)),
                            r=["shhub", "shdw"], w=[f"ps{pd}"],
                            tag=f"{tg}d{t4}_{n}_{kc}")
                    self.dve(lambda e, n=n, pd=pd, x1=x1: e.tensor_tensor(
                        out=ot[:, n * 512:(n + 1) * 512],
                        in0=ps[pd][:, 0:512],
                        in1=x1[:, n * 512:(n + 1) * 512], op=ALU.add),
                        r=[f"ps{pd}", f"x1t{b2}"], w=["sqd"],
                        tag=f"{tg}c{t4}_{n}")
                    pp = [ot[:, n * 512:(n + 1) * 512], ot2]
                    ppk = ["sqd", "ot2"]
                    for j in range(TOPK):
                        src = pp[j % 2]
                        dst = pp[(j + 1) % 2]
                        self.dve(lambda e, n=n, j=j, tt=tt, b2=b2, src=src,
                                 dst=dst: e.scalar_tensor_tensor(
                            out=dst,
                            in0=sb[f"gat{b2}"][:, j, n * 512:(n + 1) * 512],
                            scalar=sb["cw"][:, tt, j:j + 1],
                            in1=src,
                            op0=ALU.mult, op1=ALU.add),
                            r=[f"gat{b2}", "cw", ppk[j % 2]],
                            w=[ppk[(j + 1) % 2]],
                            tag=f"{tg}a{t4}_{n}_{j}")
                o = self.dma("sp", self.out.ap()[tt * 128:(tt + 1) * 128, :],
                             ot, r=["sqd"], w=["out"], tag=f"{tg}so{t4}")
                self.final.append(o)


    def build(self):
        self.build_init()
        if self.dev_route:
            self._ffn_routing()
            self.dump_raw("eki", self.sb["eki"], "xt0", F32)
            self.dump_raw("cw", self.sb["cw"], "xt0", F32)
            self.dump_raw("scall", self.sb["scores"], "xt0", F32)
            self.dump_raw("routr", self.sb["rout_rhs"], "rout_rhs", BF16)
            self.dump_raw("biase", self.sb["bias_e"], "bias_e", F32)
            self.dump_raw("io9", self.sb["io999"], "io999", F32)
            self.dump_raw("io16", self.sb["iota16"], "iota16", F32)
            self.dump_raw("srtf", self.sb["srt"], "xt0", F32)
            self.dump_raw("rtend", self.sb["rtmp"], "Rctx", F32)
            self.dma("sp", self.sb["rtmp"], self.w["router_w"].ap(),
                     r=["dbg_rtend"], w=["rtmp"], tag="reread_rw")
            self.dump_raw("rwend", self.sb["rtmp"], "Rctx", F32)
            nsig, nops = self.s.emit(self.nc, self.final)
            self.nsig, self.nops = nsig, nops
            return self.nc
        nseq = 1 if self.dev_chunks else SEQ_PER_CORE
        for seq in range(nseq):
            self.build_phase_a(seq)
        if self.dev_full:
            self.build_ffn()
            for o in self.s.ops:
                if o.tag.startswith("s1c7_sx"):
                    self.final.append(o)
            nsig, nops = self.s.emit(self.nc, self.final)
            self.nsig, self.nops = nsig, nops
            return self.nc
        if self.dev_chunks:
            sb = self.sb
            self.dump("xln_tm", sb["xln_tm"], "Rhc")
            self.dump("xln_fm", sb["xln_fm"], "Rxfm")
            self.dump("ctx", sb["ctx"], "Rctx")
            self.dump("zg", sb["zg"], "Rzg")
            self.dump("xcp", sb["xcp"], "Rxcp")
            self.dump("xdbl", sb["xdbl"], "xdbl")
            self.dump("lsg", sb["varg"], "Rvarg")
            self.dump("lsgx", sb["lsgx"], "Rlsgx")
            self.dump("nbc", sb["nbc"], "Rnbc")
            self.dump("cc", sb["cc"], "Rctx")
            self.dump("ybf", sb["ybf"], "Rsg")
            self.dump("state", sb["state"], "state")
            self.dump("hc", sb["hc"], "Rhc")
            self.dump("dAsn", self.sb["wg_t0"], "Rctx")
            self.dump("usn", self.sb["wg_t1"], "Rzg")
            self.dump("aneg", self.sb["aneg"], "aneg")

        if self.debug_phase_a:
            zs = self.sb["ybf"]
            outap = self.out.ap().rearrange("(r p) d -> p r d", p=128)
            self.dve(lambda e: e.memset(zs, 0.0), w=["ybf"], tag="zo")
            for i in range(0, TOK // 128, 8):
                o = self.dma("pool", outap[:, i:i + 8, :],
                             self.sb["ybf"]
                             .rearrange("p c q -> p (c q)")[:, 0:8 * D_MODEL]
                             .rearrange("p (n d) -> p n d", n=8),
                             r=["ybf"], w=["out"], tag=f"zout{i}")
            self.final.append(o)
            for o in self.s.ops:
                if o.tag.startswith("s1c7_sx"):
                    self.final.append(o)
        else:
            self.build_ffn()
        nsig, nops = self.s.emit(self.nc, self.final)
        self.nsig, self.nops = nsig, nops
        return self.nc



def kernel(**inputs):
    consts = host_constants()
    mk = MK(debug_phase_a=False)
    nc = mk.build()
    in_maps = []
    x = np.ascontiguousarray(inputs["L0_attn_x"].reshape(BATCH * SEQ,
                                                         D_MODEL))
    wmap = {
        "W_in": "L0_attn_W_in", "conv_w": "L0_attn_conv_w",
        "conv_b": "L0_attn_conv_b", "W_x": "L0_attn_W_x",
        "W_dt": "L0_attn_W_dt", "b_dt": "L0_attn_b_dt",
        "A_log": "L0_attn_A_log", "D_param": "L0_attn_D_param",
        "W_out": "L0_attn_W_out", "router_w": "L0_ffn_router_w",
        "bias": "L0_ffn_bias", "gate_w": "L0_ffn_gate_w",
        "up_w": "L0_ffn_up_w", "down_w": "L0_ffn_down_w",
        "sh_gate_w": "L0_ffn_sh_gate_w", "sh_up_w": "L0_ffn_sh_up_w",
        "sh_down_w": "L0_ffn_sh_down_w",
    }
    for core in range(N_CORES):
        m = {"x_in": x[core * TOK:(core + 1) * TOK]}
        for k, v in wmap.items():
            m[k] = np.ascontiguousarray(inputs[v])
        m.update(consts)
        in_maps.append(m)
    res = run_bass_kernel_spmd(nc, in_maps, core_ids=list(range(N_CORES)))
    out = np.concatenate([np.asarray(r["out"]) for r in res.results], axis=0)
    return out.reshape(BATCH, SEQ, D_MODEL).astype(np.float32)
